# version 09
# baseline (speedup 1.0000x reference)
import numpy as np
from contextlib import ExitStack

import concourse.bass as bass
import concourse.mybir as mybir

F32 = mybir.dt.float32
FR = mybir.dt.float32r
BF = mybir.dt.bfloat16
AF = mybir.ActivationFunctionType
ALU = mybir.AluOpType

D, DI, DTR, NS, FF = 1024, 2048, 64, 16, 4096
EPS = 1e-5
XCH, DCH = D // 128, DI // 128
FCH2 = FF // 128
NCORES = 8
ENGINES = ("sync", "vector", "scalar", "tensor", "gpsimd")
SBUF_BYTES = 211936


class Plan:
    def __init__(self, nc, sems, mksem):
        self.nc = nc
        self.streams = {e: [] for e in ENGINES}
        self.sems = dict(sems)
        self.mksem = mksem
        self.counts = {e: 0 for e in ENGINES}
        self.dma_counts = {}
        self.waited = {}

    def _dma_sem(self, key):
        if key not in self.sems:
            self.sems[key] = self.mksem(key)
            self.dma_counts[key] = 0
        return self.sems[key]

    def wait(self, eng, ev):
        if ev is None:
            return
        pe, val = ev
        if pe == eng or val <= 0 or self.waited.get((eng, pe), 0) >= val:
            return
        self.waited[(eng, pe)] = val
        sem = self.sems[pe]
        self.streams[eng].append(lambda E, sem=sem, val=val: E.wait_ge(sem, val))

    def do(self, eng, fn, waits=(), inc=False):
        for ev in waits:
            if ev:
                self.wait(eng, ev)
        if inc:
            self.counts[eng] += 1
            tgt = self.counts[eng]
            sem = self.sems[eng]

            def closure(E, fn=fn, sem=sem):
                fn().then_inc(sem, 1)
            self.streams[eng].append(closure)
            return (eng, tgt)
        self.streams[eng].append(lambda E, fn=fn: fn())
        return None

    def dma(self, eng, out, in_, waits=(), slow=False, key=None, tr=False):
        assert key is not None, "dma needs a stream key"
        sem = self._dma_sem(key)
        for ev in waits:
            if ev:
                self.wait(eng, ev)
        self.dma_counts[key] += 16
        tgt = self.dma_counts[key]

        def closure(E, out=out, in_=in_, sem=sem, slow=slow, tr=tr):
            if tr:
                E.dma_start_transpose(out=out, in_=in_).then_inc(sem, 16)
                return
            kw = {"allow_slow_non_contiguous": True} if slow else {}
            E.dma_start(out=out, in_=in_, **kw).then_inc(sem, 16)
        self.streams[eng].append(closure)
        return (key, tgt)

    def barrier(self):
        for key, cnt in list(self.dma_counts.items()):
            self.wait("sync", (key, cnt))
        targets = {}
        for e in ENGINES:
            self.counts[e] += 1
            targets[e] = self.counts[e]
            sem = self.sems[e]
            if e == "sync":
                self.streams[e].append(lambda E, sem=sem: E.sem_inc(sem, 1))
            else:
                self.streams[e].append(
                    lambda E, sem=sem: E.drain().then_inc(sem, 1))
        for e in ENGINES:
            for pe in ENGINES:
                if pe != e:
                    self.wait(e, (pe, targets[pe]))


class MK:
    def __init__(self, cfg):
        self.cfg = cfg
        self.L = cfg["L"]
        self.T1 = cfg["T1"]; self.TS = cfg["TS"]; self.T3 = cfg["T3"]
        self.upto = cfg.get("upto", 3)
        self.nc = bass.Bass("TRN2", target_bir_lowering=False, detect_race_conditions=False)

    def _sb_at(self, name, shape, dtype, off):
        nc = self.nc
        h = nc.virtual_sbuf_tensor(name, list(shape), dtype)
        nc._tensor(h.name, h.shape, h.dtype, type="SB")
        mloc = nc.lookup_mloc(h)
        mloc.allocated = True
        mloc.addr = self._arena_addr + off
        return h.ap()

    def persist(self, name, shape, dtype=F32):
        per = int(np.prod(shape[1:])) * mybir.dt.size(dtype)
        off = -(-self._persist_cur // 32) * 32
        self._persist_cur = off + per
        assert self._persist_cur <= self._persist_limit, f"persist overflow: {name}"
        return self._sb_at(name, shape, dtype, off)

    def tile(self, name, shape, dtype=FR):
        per = int(np.prod(shape[1:])) * mybir.dt.size(dtype)
        off = -(-self._phase_cur // 32) * 32
        self._phase_cur = off + per
        assert self._phase_cur <= SBUF_BYTES, f"sbuf overflow: {name} @{self._phase_cur}"
        return self._sb_at(name, shape, dtype, off)

    def new_phase(self):
        self._phase_cur = self._persist_limit

    def alloc_stats(self, T):
        self.stat_mu = self.tile("stat_mu", [1, T], FR)
        self.stat_ms = self.tile("stat_ms", [1, T], FR)
        self.stat_r = self.tile("stat_r", [1, T], FR)
        self.stat_nmr = self.tile("stat_nmr", [1, T], FR)
        self.stat_vk = self.tile("stat_vk", [1, T], FR)
        self.stat_tn = self.tile("stat_tn", [1, T], FR)

    def pidx(self, pbap):
        return 7 if pbap is self.pbr else self.pb.index(pbap)

    def pwaits(self, pbap):
        return list(self.pfree[self.pidx(pbap)])

    def pset(self, pbap, ev):
        self.pfree[self.pidx(pbap)] = list(ev) if isinstance(ev, list) else [ev]

    def build(self):
        nc, cfg = self.nc, self.cfg
        L = self.L

        dr = lambda n, sh, dt=FR, kind="ExternalInput": \
            nc.dram_tensor(n, sh, dt, kind=kind).ap()
        self.x_h = dr("x", [L, D])
        self.ln1_g = dr("ln1_g", [D], F32); self.ln1_b = dr("ln1_b", [D], F32)
        self.w_in = dr("w_in", [D, 2 * DI])
        self.conv_w = dr("conv_w", [DI, 4], F32); self.conv_b = dr("conv_b", [DI], F32)
        self.w_x = dr("w_x", [DI, DTR + 2 * NS])
        self.w_dt = dr("w_dt", [DTR, DI]); self.b_dt = dr("b_dt", [DI], F32)
        self.a_log = dr("a_log", [DI, NS], F32); self.d_skip = dr("d_skip", [DI], F32)
        self.w_out = dr("w_out", [DI, D])
        self.ln2_g = dr("ln2_g", [D], F32); self.ln2_b = dr("ln2_b", [D], F32)
        self.w_ff1 = dr("w_ff1", [D, 2 * FF]); self.b_ff1 = dr("b_ff1", [2 * FF], F32)
        self.w_ff2 = dr("w_ff2", [FF, D]); self.b_ff2 = dr("b_ff2", [D], F32)
        self.fw = dr("fw", [D, D]); self.fa1 = dr("fa1", [D, D]); self.fa2 = dr("fa2", [D, D])
        self.fbias = dr("fbias", [D], F32)
        self.c_vec = dr("c_vec", [D], F32)
        self.wmod = dr("wmod", [D, 6 * D])
        self.eye_h = dr("eye_h", [128, 128])
        self.out_h = dr("out", [L, D], F32, kind="ExternalOutput")

        dumps = set(cfg.get("dump", ()))
        def scr(n, sh, dt=FR):
            kind = "ExternalOutput" if n in dumps else "Internal"
            return nc.dram_tensor(n, sh, dt, kind=kind).ap()
        self.ha_sp = scr("ha_sp", [128, XCH * L])
        self.xc_sp = scr("xc_sp", [128, DCH * L])
        self.dt_sp = scr("dt_sp", [128, DCH * L])
        self.sz_sp = scr("sz_sp", [128, DCH * L])
        self.bc_sp = scr("bc_sp", [2 * NS, L])
        self.y3_sp = scr("y3_sp", [128, DCH * L], BF)
        self.wc_sp = scr("wc_sp", [D, D], BF)
        self.wo_sp = scr("wo_sp", [DI, D], BF)
        self.xin2_sp = scr("xin2_sp", [128, XCH * L])
        self.pre_sp = scr("pre_sp", [128, XCH * L])
        self.h2_sp = scr("h2_sp", [128, XCH * L], BF)
        self.ag_sp = scr("ag_sp", [128, FCH2 * L], BF)
        self.x2_sp = scr("x2_sp", [128, XCH * L])
        self.wf1_sp = scr("wf1_sp", [D, 2 * FF], BF)
        self.wi_sp = scr("wi_sp", [D, 2 * DI], BF)
        self.wf2_sp = scr("wf2_sp", [FF, D], BF)
        self.halo_sp = scr("halo_sp", [128, DCH * 4])
        self.zero_sp = scr("zero_sp", [128, DCH * NS])
        self.y33_sp = scr("y33_sp", [128, DCH * 256], BF)

        with ExitStack() as st:
            arena = st.enter_context(
                nc.sbuf_tensor("arena", [128, SBUF_BYTES], mybir.dt.uint8))
            self._arena_addr = nc.lookup_mloc(arena).addr
            self._persist_cur = 0
            self._persist_limit = 8 * 1024
            self._phase_cur = self._persist_limit

            self.pb = [st.enter_context(nc.psum_tensor(f"pb{i}", [128, 512], F32)).ap()
                       for i in range(7)]
            self.pbr = st.enter_context(nc.psum_tensor("pbr", [128, 512], FR)).ap()
            sems = {e: st.enter_context(nc.semaphore("s_" + e)) for e in ENGINES}
            mksem = lambda name: st.enter_context(nc.semaphore("d_" + name))
            self.p = Plan(nc, sems, mksem)
            self.pfree = [[] for _ in range(8)]

            self.alloc_persist()
            self.phase0()
            self.p.barrier()
            self.phase0b()
            self.p.barrier()
            if self.upto >= 1:
                self.phase1()
                self.p.barrier()
            if self.upto == 2:
                for _ in self.phase2():
                    pass
                self.p.barrier()
            elif self.upto >= 3:
                g2, g3 = self.phase2(), self.phase3()
                CPB = self.TS // self.T1
                done3 = 0
                for b in g2:
                    pass_through = None
                    while done3 + 1 <= (b + 1) * CPB:
                        next(g3)
                        done3 += 1
                for _ in g3:
                    pass
            self.p.barrier()

            blk = st.enter_context(nc.Block())
            for e in ENGINES:
                fns = self.p.streams[e]

                def runner(E, fns=fns):
                    for f in fns:
                        f(E)
                getattr(blk, e)(runner)
        return nc

    def alloc_persist(self):
        p = self.persist
        self.ln1g_f = p("ln1g_f", [128, XCH]); self.ln1b_f = p("ln1b_f", [128, XCH])
        self.ln2g_f = p("ln2g_f", [128, XCH]); self.ln2b_f = p("ln2b_f", [128, XCH])
        self.convw_f = p("convw_f", [128, DCH * 4]); self.convb_f = p("convb_f", [128, DCH])
        self.bdt_f = p("bdt_f", [128, DCH]); self.dskip_f = p("dskip_f", [128, DCH])
        self.A_f = p("A_f", [128, DCH * NS])
        self.bff1_f = p("bff1_f", [128, 2 * FCH2]); self.bff2_f = p("bff2_f", [128, XCH])
        self.fbias_f = p("fbias_f", [128, XCH])
        self.cvec_f = p("cvec_f", [128, XCH]); self.cvec_r = p("cvec_r", [128, XCH], FR)
        self.mod_f = p("mod_f", [128, 6 * XCH])
        self.eye_r = p("eye_r", [128, 128], FR)
        self.eye_f = p("eye_f", [128, 128], F32)
        self.ones_col = p("ones_col", [128, 1], FR)
        self.ones_row = p("ones_row", [1, 128], FR)
        self.ones_cf = p("ones_cf", [128, 1], F32)
        self.zero_f = p("zero_f", [128, DCH * NS], F32)
        self.ones_rf = p("ones_rf", [1, 128], F32)
        self.state = p("state", [128, DCH * NS], FR)
        self.halo = p("halo", [128, DCH * 4], FR)

    def layer_norm(self, T, v_ap, dst_ap, g_col, b_col, vready, tmp, sq, rbc, nbc,
                   banks=(4, 5, 2, 3), dst_free=()):
        nc, p = self.nc, self.p
        fr = getattr(self, "_lnfree", {})
        self._lnfree = fr
        pb_mu, pb_ms, pb_r, pb_n = (self.pb[i] for i in banks)
        sev = p.do("scalar", lambda: nc.scalar.activation(
            out=sq, in_=v_ap, func=AF.Square), waits=[vready, fr.get("sq")], inc=True)
        mu_ev = ms_ev = None
        for pb, rhs_ap, wev in ((pb_mu, v_ap, vready), (pb_ms, sq, sev)):
            ev = None
            for kc in range(XCH):
                rhs = rhs_ap[:, kc * T:(kc + 1) * T]
                ev = p.do("tensor", lambda pb=pb, rhs=rhs, kc=kc: nc.tensor.matmul(
                    out=pb[0:1, 0:T], lhsT=self.ones_col, rhs=rhs,
                    start=(kc == 0), stop=(kc == XCH - 1)),
                    waits=[wev] + (self.pwaits(pb) if kc == 0 else []),
                    inc=(kc == XCH - 1))
            if pb is pb_mu:
                mu_ev = ev
            else:
                ms_ev = ev
        fr["sq"] = ms_ev
        smu, sms = self.stat_mu[:, 0:T], self.stat_ms[:, 0:T]
        sr, snm = self.stat_r[:, 0:T], self.stat_nmr[:, 0:T]
        p.do("vector", lambda: nc.vector.tensor_scalar(
            out=smu, in0=pb_mu[0:1, 0:T], scalar1=1.0 / D, scalar2=None, op0=ALU.mult),
            waits=[mu_ev, fr.get("stat")])
        ev = p.do("vector", lambda: nc.vector.tensor_scalar(
            out=sms, in0=pb_ms[0:1, 0:T], scalar1=1.0 / D, scalar2=EPS,
            op0=ALU.mult, op1=ALU.add), waits=[ms_ev], inc=True)
        self.pset(pb_mu, ev); self.pset(pb_ms, ev)
        p.do("vector", lambda: nc.vector.tensor_tensor(
            out=sr, in0=smu, in1=smu, op=ALU.mult))
        sub_ev = p.do("vector", lambda: nc.vector.tensor_sub(
            out=sms, in0=sms, in1=sr), inc=True)
        vkeep = self.stat_vk[:, 0:T]
        sqe = p.do("scalar", lambda: nc.scalar.activation(
            out=vkeep, in_=sms, func=AF.Sqrt), waits=[sub_ev], inc=True)
        def _recip():
            with nc.allow_low_precision("f32r == fp32 precision"):
                return nc.vector.reciprocal(out=sr, in_=vkeep)
        p.do("vector", _recip, waits=[sqe])
        tnr = self.stat_tn[:, 0:T]
        for _ in range(2):
            p.do("vector", lambda: nc.vector.tensor_mul(out=tnr, in0=sr, in1=sr))
            p.do("vector", lambda: nc.vector.tensor_mul(out=tnr, in0=tnr, in1=sms))
            p.do("vector", lambda: nc.vector.tensor_scalar(
                out=tnr, in0=tnr, scalar1=-0.5, scalar2=1.5,
                op0=ALU.mult, op1=ALU.add))
            p.do("vector", lambda: nc.vector.tensor_mul(out=sr, in0=sr, in1=tnr))
        rev = p.do("vector", lambda: nc.vector.engine_nop(), inc=True)
        p.do("vector", lambda: nc.vector.tensor_tensor(
            out=snm, in0=smu, in1=sr, op=ALU.mult))
        nev = p.do("vector", lambda: nc.vector.tensor_scalar(
            out=snm, in0=snm, scalar1=-1.0, scalar2=None, op0=ALU.mult), inc=True)
        bev1 = p.do("tensor", lambda: nc.tensor.matmul(
            out=pb_r[:, 0:T], lhsT=self.ones_row, rhs=sr, start=True, stop=True),
            waits=[rev] + self.pwaits(pb_r), inc=True)
        bev2 = p.do("tensor", lambda: nc.tensor.matmul(
            out=pb_n[:, 0:T], lhsT=self.ones_row, rhs=snm, start=True, stop=True),
            waits=[nev] + self.pwaits(pb_n), inc=True)
        fr["stat"] = bev2
        ce1 = p.do("scalar", lambda: nc.scalar.copy(out=rbc, in_=pb_r[:, 0:T]),
                   waits=[bev1, fr.get("rbc")], inc=True)
        ce2 = p.do("scalar", lambda: nc.scalar.copy(out=nbc, in_=pb_n[:, 0:T]),
                   waits=[bev2, fr.get("rbc")], inc=True)
        self.pset(pb_r, ce1); self.pset(pb_n, ce2)
        last = None
        for kc in range(XCH):
            vv = v_ap[:, kc * T:(kc + 1) * T]
            dd = dst_ap[:, kc * T:(kc + 1) * T]
            p.do("vector", lambda vv=vv: nc.vector.tensor_mul(
                out=tmp, in0=vv, in1=rbc), waits=[ce1, ce2] + list(dst_free))
            p.do("vector", lambda: nc.vector.tensor_add(out=tmp, in0=tmp, in1=nbc))
            last = p.do("vector", lambda kc=kc, dd=dd: nc.vector.tensor_scalar(
                out=dd, in0=tmp, scalar1=g_col(kc), scalar2=b_col(kc),
                op0=ALU.mult, op1=ALU.add), inc=(kc == XCH - 1))
        fr["rbc"] = last
        return last, ms_ev

    def phase0(self):
        nc, p = self.nc, self.p
        self.new_phase()
        loads = [(self.ln1g_f, self.ln1_g, D), (self.ln1b_f, self.ln1_b, D),
                 (self.ln2g_f, self.ln2_g, D), (self.ln2b_f, self.ln2_b, D),
                 (self.convb_f, self.conv_b, DI), (self.bdt_f, self.b_dt, DI),
                 (self.dskip_f, self.d_skip, DI), (self.bff2_f, self.b_ff2, D),
                 (self.fbias_f, self.fbias, D), (self.bff1_f, self.b_ff1, 2 * FF),
                 (self.cvec_f, self.c_vec, D)]
        tot = sum(n for _, _, n in loads)
        vrow = self.tile("vrow", [1, tot], F32)
        off = 0
        for dst, src, n in loads:
            p.dma("sync", vrow[:, off:off + n], src[None, :], key="sv")
            off += n
        for j in range(DCH):
            p.dma("sync", self.convw_f[:, j * 4:(j + 1) * 4],
                  self.conv_w[j * 128:(j + 1) * 128, :], key="sv")
            p.dma("sync", self.A_f[:, j * NS:(j + 1) * NS],
                  self.a_log[j * 128:(j + 1) * 128, :], key="sv")
        sv_done = p.dma("sync", self.eye_r, self.eye_h, key="sv")
        eye_ev = p.do("vector", lambda: nc.vector.tensor_copy(
            out=self.eye_f, in_=self.eye_r), waits=[sv_done], inc=True)
        pbv = self.pb[3]
        vec_ev = {}
        off = 0
        for dst, src, n in loads:
            n128 = n // 128
            tev = None
            for i in range(n128):
                tev = p.do("tensor", lambda off=off, i=i: nc.tensor.transpose(
                    out=pbv[:, i:i + 1],
                    in_=vrow[0:1, off + i * 128:off + (i + 1) * 128],
                    identity=self.eye_f[0:1, 0:1]),
                    waits=[sv_done, eye_ev] + (self.pwaits(pbv) if i == 0 else []),
                    inc=(i == n128 - 1))
            cev = p.do("vector", lambda dst=dst, n128=n128: nc.vector.tensor_copy(
                out=dst[:, 0:n128], in_=pbv[:, 0:n128]), waits=[tev], inc=True)
            self.pset(pbv, cev)
            vec_ev[id(dst)] = cev
            off += n
        p.do("vector", lambda: nc.vector.memset(self.ones_cf, 1.0))
        p.do("vector", lambda: nc.vector.memset(self.ones_rf, 1.0))
        p.do("vector", lambda: nc.vector.tensor_copy(out=self.ones_col, in_=self.ones_cf))
        p.do("vector", lambda: nc.vector.tensor_copy(out=self.ones_row, in_=self.ones_rf))
        p.do("vector", lambda: nc.vector.memset(self.zero_f, 0.0))
        p.do("vector", lambda: nc.vector.tensor_copy(
            out=self.halo, in_=self.zero_f[:, 0:DCH * 4]))
        p.do("vector", lambda: nc.vector.tensor_copy(
            out=self.state, in_=self.zero_f), inc=True)
        p.do("scalar", lambda: nc.scalar.activation(
            out=self.A_f, in_=self.A_f, func=AF.Exp), waits=[sv_done])
        p.do("scalar", lambda: nc.scalar.mul(out=self.A_f, in_=self.A_f, mul=-1.0),
             inc=True)
        sev = p.do("scalar", lambda: nc.scalar.activation(
            out=self.cvec_f, in_=self.cvec_f, func=AF.Silu),
            waits=[sv_done, vec_ev[id(self.cvec_f)]], inc=True)
        cbe = p.do("vector", lambda: nc.vector.tensor_copy(
            out=self.cvec_r, in_=self.cvec_f), waits=[sev], inc=True)

        mod_row = self.tile("mod_row", [1, 6 * D], F32)
        wm_sl = [self.tile(f"wm{i}", [128, 512], FR) for i in range(2)]
        slot_ev = [None, None]
        eev = None
        for g in range(12):
            pbk = self.pb[g % 2]
            col0 = g * 512
            mev = None
            for kc in range(XCH):
                sl = wm_sl[kc % 2]
                lev = p.dma("sync", sl,
                            self.wmod[kc * 128:(kc + 1) * 128, col0:col0 + 512],
                            waits=[slot_ev[kc % 2]], key=f"wm{kc % 2}")
                mev = p.do("tensor", lambda pbk=pbk, sl=sl, kc=kc: nc.tensor.matmul(
                    out=pbk[0:1, :], lhsT=self.cvec_r[:, kc:kc + 1], rhs=sl,
                    start=(kc == 0), stop=(kc == XCH - 1)),
                    waits=[lev, cbe] + (self.pwaits(pbk) if kc == 0 else []),
                    inc=True)
                slot_ev[kc % 2] = mev
            eev = p.do("vector", lambda pbk=pbk, col0=col0: nc.vector.tensor_copy(
                out=mod_row[:, col0:col0 + 512], in_=pbk[0:1, :]), waits=[mev], inc=True)
            self.pset(pbk, eev)
        rsgn = self.tile("rsgn", [1, 2 * D], F32)
        rabs = self.tile("rabs", [1, 2 * D], F32)
        rowv = lambda v: mod_row[:, v * D:(v + 1) * D]
        p.do("vector", lambda: nc.vector.tensor_scalar(
            out=rowv(1), in0=rowv(1), scalar1=1.0, scalar2=None, op0=ALU.add),
            waits=[eev])
        p.do("vector", lambda: nc.vector.tensor_scalar(
            out=rowv(4), in0=rowv(4), scalar1=1.0, scalar2=None, op0=ALU.add))
        for i, v in enumerate((2, 5)):
            p.do("vector", lambda i=i, v=v: nc.vector.tensor_scalar(
                out=rsgn[:, i * D:(i + 1) * D], in0=rowv(v), scalar1=0.0,
                scalar2=None, op0=ALU.is_ge))
            ab = p.do("scalar", lambda i=i, v=v: nc.scalar.activation(
                out=rabs[:, i * D:(i + 1) * D], in_=rowv(v), func=AF.Abs),
                waits=[eev], inc=True)
        sg_ev = p.do("vector", lambda: nc.vector.tensor_scalar(
            out=rsgn, in0=rsgn, scalar1=2.0, scalar2=-1.0, op0=ALU.mult,
            op1=ALU.add), waits=[ab], inc=True)
        p.do("vector", lambda: nc.vector.tensor_scalar_max(
            out=rabs, in0=rabs, scalar1=0.25))
        cl_ev = p.do("vector", lambda: nc.vector.tensor_mul(
            out=rsgn, in0=rsgn, in1=rabs), inc=True)
        for i, v in enumerate((2, 5)):
            p.do("vector", lambda i=i, v=v: nc.vector.tensor_copy(
                out=rowv(v), in_=rsgn[:, i * D:(i + 1) * D]))
        row_done = p.do("vector", lambda: nc.vector.engine_nop(), inc=True)
        pbm = self.pb[2]
        tevs = []
        for i in range(6 * XCH):
            tevs.append(p.do("tensor", lambda i=i: nc.tensor.transpose(
                out=pbm[:, i:i + 1], in_=mod_row[0:1, i * 128:(i + 1) * 128],
                identity=self.eye_f[0:1, 0:1]),
                waits=[eev, row_done] + (self.pwaits(pbm) if i == 0 else []),
                inc=(i == 6 * XCH - 1)))
        mfe = p.do("vector", lambda: nc.vector.tensor_copy(
            out=self.mod_f, in_=pbm[:, 0:6 * XCH]), waits=[tevs[-1]], inc=True)
        self.pset(pbm, mfe)

        wprep = self.tile("wprep", [128, 2 * FF], FR)
        wprep_b = self.tile("wprep_b", [128, 2 * FF], BF)

        cast_chain = [None, None]

        def cast_weight(src, dst, K, M, swz=False):
            nk = -(-K // 128)
            for kc in range(nk):
                kk = min(128, K - kc * 128)
                lev = p.dma("sync", wprep[0:kk, 0:M],
                            src[kc * 128:kc * 128 + kk, :],
                            waits=[cast_chain[0]], key="cast")
                ce = p.do("vector", lambda kk=kk, M=M: nc.vector.tensor_copy(
                    out=wprep_b[0:kk, 0:M], in_=wprep[0:kk, 0:M]),
                    waits=[lev, cast_chain[1]], inc=True)
                if swz:
                    nc_blk = M // 128
                    dv = dst.rearrange("a b -> (a b)").rearrange(
                        "(c q k m) -> q c k m", q=128, k=nk, m=128)
                    sv = wprep_b.rearrange("q (c m) -> q c m", m=128)
                    for c4 in range(4):
                        cs = nc_blk // 4
                        sev = p.dma("gpsimd",
                                    dv[:, c4 * cs:(c4 + 1) * cs, kc, :],
                                    sv[:, c4 * cs:(c4 + 1) * cs, :],
                                    waits=[ce], key="castst")
                else:
                    sev = p.dma("gpsimd", dst[kc * 128:kc * 128 + kk, :],
                                wprep_b[0:kk, 0:M], waits=[ce], key="castst")
                cast_chain[0] = ce
                cast_chain[1] = sev
            return sev

        cast_weight(self.w_ff1, self.wf1_sp, D, 2 * FF, swz=True)
        cast_weight(self.w_ff2, self.wf2_sp, FF, D)
        cast_weight(self.w_out, self.wo_sp, DI, D)
        cast_weight(self.w_in, self.wi_sp, D, 2 * DI)

    def phase0b(self):
        nc, p = self.nc, self.p
        self.new_phase()
        slab = self.tile("p0slab", [128, D], FR)
        a1t = self.tile("a1t", [128, XCH * D], FR)
        h_sb = self.tile("h_sb", [128, XCH * D], FR)
        wt_sb = self.tile("wt_sb", [128, XCH * D], FR)
        a2_sb = self.tile("a2_sb", [128, XCH * D], FR)
        wc_row = self.tile("wc_row", [128, D], BF)

        def transpose_in(src_h, dst_sb):
            ev = None
            for rc in range(XCH):
                lev = p.dma("sync", slab, src_h[rc * 128:(rc + 1) * 128, :],
                            waits=[ev], key="slab")
                for cc in range(XCH):
                    tev = p.do("tensor", lambda cc=cc: nc.tensor.transpose(
                        out=self.pbr[:, 0:128],
                        in_=slab[:, cc * 128:(cc + 1) * 128], identity=self.eye_r),
                        waits=[lev] + self.pwaits(self.pbr), inc=True)
                    ev = p.do("scalar", lambda rc=rc, cc=cc: nc.scalar.copy(
                        out=dst_sb[:, cc * D + rc * 128:cc * D + (rc + 1) * 128],
                        in_=self.pbr[:, 0:128]), waits=[tev], inc=True)
                    self.pset(self.pbr, ev)
            return ev

        ta1 = transpose_in(self.fa1, a1t)
        a2ev = None
        for kc in range(XCH):
            a2ev = p.dma("sync", a2_sb[:, kc * D:(kc + 1) * D],
                         self.fa2[kc * 128:(kc + 1) * 128, :], key="a2")
        for k1c in range(XCH):
            for jh in range(2):
                pbk = self.pb[jh]
                mev = None
                for k2c in range(XCH):
                    mev = p.do("tensor", lambda pbk=pbk, k1c=k1c, jh=jh, k2c=k2c:
                               nc.tensor.matmul(
                        out=pbk[:, :],
                        lhsT=a1t[:, k2c * D + k1c * 128:k2c * D + (k1c + 1) * 128],
                        rhs=a2_sb[:, k2c * D + jh * 512:k2c * D + jh * 512 + 512],
                        start=(k2c == 0), stop=(k2c == XCH - 1)),
                        waits=[a2ev, ta1] + (self.pwaits(pbk) if k2c == 0 else []),
                        inc=(k2c == XCH - 1))
                eev = p.do("scalar", lambda pbk=pbk, k1c=k1c, jh=jh: nc.scalar.copy(
                    out=h_sb[:, k1c * D + jh * 512:k1c * D + jh * 512 + 512],
                    in_=pbk[:, :]), waits=[mev], inc=True)
                self.pset(pbk, eev)
        h_ev = eev
        twt = transpose_in(self.fw, wt_sb)
        for ic in range(XCH):
            sev = None
            for jh in range(2):
                pbk = self.pb[jh]
                mev = None
                for k1c in range(XCH):
                    mev = p.do("tensor", lambda pbk=pbk, ic=ic, jh=jh, k1c=k1c:
                               nc.tensor.matmul(
                        out=pbk[:, :],
                        lhsT=wt_sb[:, k1c * D + ic * 128:k1c * D + (ic + 1) * 128],
                        rhs=h_sb[:, k1c * D + jh * 512:k1c * D + jh * 512 + 512],
                        start=(k1c == 0), stop=(k1c == XCH - 1)),
                        waits=[h_ev, twt] + (self.pwaits(pbk) if k1c == 0 else []),
                        inc=(k1c == XCH - 1))
                eev = p.do("vector", lambda pbk=pbk, jh=jh: nc.vector.tensor_copy(
                    out=wc_row[:, jh * 512:jh * 512 + 512], in_=pbk[:, :]),
                    waits=[mev] + ([sev] if jh == 0 else []), inc=True)
                self.pset(pbk, eev)
            sev = p.dma("gpsimd", self.wc_sp[ic * 128:(ic + 1) * 128, :], wc_row,
                        waits=[eev], key="wcst")
            p.wait("vector", sev)

    def phase1(self):
        nc, p = self.nc, self.p
        L, T1 = self.L, self.T1
        NCH = L // T1
        NSUB = T1 // 128
        self.new_phase()
        w_in_sb = self.tile("w_in_sb", [128, XCH * 2 * DI], BF)
        wx_sb = self.tile("wx_sb", [128, DCH * 96], FR)
        wdt_sb = self.tile("wdt_sb", [64, DI], FR)
        x_tm = self.tile("x_tm", [128, NSUB * D], FR)
        x_fm = self.tile("x_fm", [128, XCH * T1], FR)
        ha_fm = self.tile("ha_fm", [128, XCH * T1], FR)
        h1 = self.tile("h1", [128, XCH * T1], BF)
        sq = self.tile("sq1", [128, XCH * T1], FR)
        tmp = self.tile("tmp1", [128, T1], FR)
        rbc = self.tile("rbc1", [128, T1], FR)
        nbc = self.tile("nbc1", [128, T1], FR)
        xcp = [self.tile(f"xcp{i}", [128, T1 + 4], FR) for i in range(2)]
        cvt = [self.tile(f"cvt{i}", [128, T1], FR) for i in range(2)]
        xc_full = self.tile("xc_full", [128, DCH * T1], FR)
        sz_rot = [self.tile(f"szr{i}", [128, T1], FR) for i in range(2)]
        dt_rot = [self.tile(f"dtr{i}", [128, T1], FR) for i in range(2)]
        sp1 = self.tile("sp1", [128, T1], FR)
        sp2 = self.tile("sp2", [128, T1], FR)
        dtlo = self.tile("dtlo", [64, T1], FR)
        dbc32 = self.tile("dbc32", [32, T1], FR)

        self.alloc_stats(T1)
        ev_w = None
        for kc in range(XCH):
            ev_w = p.dma("sync", w_in_sb[:, kc * 2 * DI:(kc + 1) * 2 * DI],
                         self.wi_sp[kc * 128:(kc + 1) * 128, :], key="w1")
        ev_wx = None
        for kc in range(DCH):
            ev_wx = p.dma("sync", wx_sb[:, kc * 96:(kc + 1) * 96],
                          self.w_x[kc * 128:(kc + 1) * 128, :], key="wx")
        ev_wdt = p.dma("sync", wdt_sb, self.w_dt, key="wdt")

        xcsp = self.xc_sp.rearrange("q (c j t) -> q c j t", j=DCH, t=T1)
        dtsp = self.dt_sp.rearrange("q (c j t) -> q c j t", j=DCH, t=T1)
        szsp = self.sz_sp.rearrange("q (c j t) -> q c j t", j=DCH, t=T1)
        if "halo_sp" in set(self.cfg.get("dump", ())):
            p.dma("gpsimd", self.halo_sp, self.halo, key="dbgh")
            p.dma("gpsimd", self.zero_sp, self.zero_f, key="dbgh")

        x_free = None
        ha_sp_ev = None
        xcs_last = None
        xcs_last_prev = None
        cvt_ev = [None, None]
        xcp_ev = [None, None]
        sz_ev = [None, None]
        dt_ev = [None, None]
        bc_ev = None
        ln1_stats_prev = None
        win_last_prev = None
        wx_last_prev = None
        wdt_last_prev = None
        lna_stats = None
        lna_apply_prev = None
        for c in range(NCH):
            xcs_last_prev = xcs_last
            lev = None
            for s in range(NSUB):
                lev = p.dma("sync", x_tm[:, s * D:(s + 1) * D],
                            self.x_h[c * T1 + s * 128:c * T1 + (s + 1) * 128, :],
                            waits=[x_free], key="x")
            x_ev = None
            for fc in range(XCH):
                tev = None
                for s in range(NSUB):
                    tev = p.do("tensor", lambda s=s, fc=fc: nc.tensor.transpose(
                        out=self.pbr[:, s * 128:(s + 1) * 128],
                        in_=x_tm[:, s * D + fc * 128:s * D + (fc + 1) * 128],
                        identity=self.eye_r),
                        waits=[lev] + (self.pwaits(self.pbr) if s == 0 else []),
                        inc=(s == NSUB - 1))
                x_ev = p.do("scalar", lambda fc=fc: nc.scalar.copy(
                    out=x_fm[:, fc * T1:(fc + 1) * T1], in_=self.pbr[:, 0:T1]),
                    waits=[tev, lna_stats, lna_apply_prev], inc=True)
                self.pset(self.pbr, x_ev)
            x_free = x_ev
            ca = lambda kc: self.mod_f[:, 1 * XCH + kc:1 * XCH + kc + 1]
            sa = lambda kc: self.mod_f[:, 0 * XCH + kc:0 * XCH + kc + 1]
            lna_apply_prev_new = None
            ha_ev, lna_stats = self.layer_norm(
                T1, x_fm, ha_fm, ca, sa, x_ev, tmp, sq, rbc, nbc,
                dst_free=[ln1_stats_prev, ha_sp_ev])
            lna_apply_prev = ha_ev
            ha_sp_ev = p.dma("gpsimd", self.ha_sp[:, c * XCH * T1:(c + 1) * XCH * T1],
                             ha_fm, waits=[ha_ev, ha_sp_ev], key="has")
            g1 = lambda kc: self.ln1g_f[:, kc:kc + 1]
            b1 = lambda kc: self.ln1b_f[:, kc:kc + 1]
            h1_ev, ln1_stats_prev = self.layer_norm(
                T1, ha_fm, h1, g1, b1, ha_ev, tmp, sq, rbc, nbc,
                dst_free=[win_last_prev])

            for mc in range(2 * DCH):
                pbk = self.pb[mc % 2]
                mev = None
                for kc in range(XCH):
                    mev = p.do("tensor", lambda pbk=pbk, mc=mc, kc=kc: nc.tensor.matmul(
                        out=pbk[:, 0:T1],
                        lhsT=w_in_sb[:, kc * 2 * DI + mc * 128:
                                     kc * 2 * DI + (mc + 1) * 128],
                        rhs=h1[:, kc * T1:(kc + 1) * T1],
                        start=(kc == 0), stop=(kc == XCH - 1)),
                        waits=[h1_ev, ev_w] + (self.pwaits(pbk) if kc == 0 else []),
                        inc=(kc == XCH - 1))
                if mc == 2 * DCH - 1:
                    win_last_prev = mev
                r = mc % 2
                if mc < DCH:
                    xp = xcp[r]
                    if c == 0:
                        p.do("vector", lambda xp=xp: nc.vector.tensor_copy(
                            out=xp[:, 0:4], in_=self.zero_f[:, 0:4]),
                            waits=[xcp_ev[r]])
                    else:
                        p.do("vector", lambda xp=xp, mc=mc: nc.vector.tensor_copy(
                            out=xp[:, 0:3], in_=self.halo[:, mc * 4:mc * 4 + 3]),
                            waits=[xcp_ev[r]])
                    eev = p.do("vector", lambda xp=xp, pbk=pbk: nc.vector.tensor_copy(
                        out=xp[:, 3:3 + T1], in_=pbk[:, 0:T1]), waits=[mev], inc=True)
                    hev = p.do("vector", lambda pbk=pbk, mc=mc: nc.vector.tensor_copy(
                        out=self.halo[:, mc * 4:mc * 4 + 3],
                        in_=pbk[:, T1 - 3:T1]), inc=True)
                    self.pset(pbk, [eev, hev])
                    cv = cvt[r]
                    wcol = lambda k, mc=mc: self.convw_f[:, mc * 4 + k:mc * 4 + k + 1]
                    p.do("vector", lambda cv=cv, xp=xp, wcol=wcol:
                         nc.vector.tensor_scalar(
                             out=cv, in0=xp[:, 0:T1], scalar1=wcol(0), scalar2=None,
                             op0=ALU.mult), waits=[cvt_ev[r]])
                    for k in (1, 2, 3):
                        p.do("vector", lambda cv=cv, xp=xp, k=k, wcol=wcol:
                             nc.vector.scalar_tensor_tensor(
                                 out=cv, in0=xp[:, k:k + T1], scalar=wcol(k), in1=cv,
                                 op0=ALU.mult, op1=ALU.add))
                    ve = p.do("vector", lambda: nc.vector.engine_nop(), inc=True)
                    xcp_ev[r] = ve
                    se = p.do("scalar", lambda cv=cv, mc=mc: nc.scalar.activation(
                        out=xc_full[:, mc * T1:(mc + 1) * T1], in_=cv, func=AF.Silu,
                        bias=self.convb_f[:, mc:mc + 1]),
                        waits=[ve, wx_last_prev, xcs_last_prev], inc=True)
                    cvt_ev[r] = se
                    xcs_last = p.dma("gpsimd", xcsp[:, c, mc, :],
                                     xc_full[:, mc * T1:(mc + 1) * T1],
                                     waits=[se], key="xcs")
                else:
                    j = mc - DCH
                    se = p.do("scalar", lambda r=r, pbk=pbk: nc.scalar.activation(
                        out=sz_rot[r], in_=pbk[:, 0:T1], func=AF.Silu),
                        waits=[mev, sz_ev[r]], inc=True)
                    self.pset(pbk, se)
                    spe = p.dma("gpsimd", szsp[:, c, j, :], sz_rot[r],
                                waits=[se], key=f"szs{r}")
                    sz_ev[r] = spe
            pbk = self.pb[2]
            mev = None
            for kc in range(DCH):
                mev = p.do("tensor", lambda pbk=pbk, kc=kc: nc.tensor.matmul(
                    out=pbk[0:96, 0:T1], lhsT=wx_sb[:, kc * 96:(kc + 1) * 96],
                    rhs=xc_full[:, kc * T1:(kc + 1) * T1],
                    start=(kc == 0), stop=(kc == DCH - 1)),
                    waits=[cvt_ev[0], cvt_ev[1], ev_wx] +
                          (self.pwaits(pbk) if kc == 0 else []),
                    inc=(kc == DCH - 1))
            wx_last_prev = mev
            e1 = p.do("vector", lambda pbk=pbk: nc.vector.tensor_copy(
                out=dtlo, in_=pbk[0:64, 0:T1]), waits=[mev, wdt_last_prev], inc=True)
            e2 = p.do("vector", lambda pbk=pbk: nc.vector.tensor_copy(
                out=dbc32, in_=pbk[64:96, 0:T1]), waits=[bc_ev], inc=True)
            self.pset(pbk, e2)
            bc_ev = p.dma("gpsimd", self.bc_sp[:, c * T1:(c + 1) * T1], dbc32,
                          waits=[e2], key="bcs")
            for mc in range(DCH):
                pbk = self.pb[3 + mc % 2]
                r = mc % 2
                mev = p.do("tensor", lambda pbk=pbk, mc=mc: nc.tensor.matmul(
                    out=pbk[:, 0:T1], lhsT=wdt_sb[:, mc * 128:(mc + 1) * 128],
                    rhs=dtlo, start=True, stop=True),
                    waits=[e1, ev_wdt] + self.pwaits(pbk), inc=True)
                if mc == DCH - 1:
                    wdt_last_prev = mev
                p.do("vector", lambda pbk=pbk, mc=mc: nc.vector.tensor_scalar(
                    out=sp1, in0=pbk[:, 0:T1], scalar1=self.bdt_f[:, mc:mc + 1],
                    scalar2=None, op0=ALU.add), waits=[mev, dt_ev[r]])
                ee = p.do("vector", lambda: nc.vector.engine_nop(), inc=True)
                self.pset(pbk, ee)
                p.do("scalar", lambda: nc.scalar.activation(
                    out=sp2, in_=sp1, func=AF.Abs), waits=[ee])
                p.do("scalar", lambda: nc.scalar.activation(
                    out=sp2, in_=sp2, func=AF.Exp, scale=-1.0))
                se = p.do("scalar", lambda: nc.scalar.activation(
                    out=sp2, in_=sp2, func=AF.Ln, bias=1.0), inc=True)
                p.do("vector", lambda: nc.vector.tensor_scalar_max(
                    out=sp1, in0=sp1, scalar1=0.0), waits=[se])
                de = p.do("vector", lambda r=r: nc.vector.tensor_add(
                    out=dt_rot[r], in0=sp1, in1=sp2), inc=True)
                spe = p.dma("gpsimd", dtsp[:, c, mc, :], dt_rot[r],
                            waits=[de], key=f"dts{r}")
                dt_ev[r] = spe

    def phase2(self):
        nc, p = self.nc, self.p
        L, TS = self.L, self.TS
        NB = L // TS
        self.new_phase()
        dt_t = [self.tile(f"p2dt{i}", [128, TS], FR) for i in range(2)]
        xc_t = [self.tile(f"p2xc{i}", [128, TS], FR) for i in range(2)]
        sz_t = [self.tile(f"p2sz{i}", [128, TS], FR) for i in range(2)]
        dtx = self.tile("p2dtx", [128, TS], FR)
        a_t = [self.tile(f"p2a{i}", [128, TS], FR) for i in range(2)]
        u_t = [self.tile(f"p2u{i}", [128, TS], FR) for i in range(2)]
        bcrow = self.tile("p2bcrow", [1, 4 * TS], FR)
        bbc = self.tile("p2bbc", [128, NS * TS], FR)
        cint = self.tile("p2cint", [128, NS * TS], FR)
        h_int = [self.tile("p2h0", [128, NS * TS], FR)] * 2
        y_t = self.tile("p2y", [128, TS], F32)
        y3_t = [self.tile(f"p2y3{i}", [128, TS], BF) for i in range(2)]

        T1 = self.T1
        assert TS % T1 == 0
        CPB = TS // T1
        self._y3done = {}
        xcsp = self.xc_sp.rearrange("q (c j t) -> q c j t", j=DCH, t=T1)
        dtsp = self.dt_sp.rearrange("q (c j t) -> q c j t", j=DCH, t=T1)
        szsp = self.sz_sp.rearrange("q (c j t) -> q c j t", j=DCH, t=T1)
        y3sp = self.y3_sp.rearrange("q (c j t) -> q c j t", j=DCH, t=T1)

        dt_free = [None, None]; xc_free = [None, None]; sz_free = [None, None]
        au_free = [[None, None], [None, None]]
        h_free = [None, None]
        y3s_ev = [None, None]
        bc_free = None
        y3e = None
        cm_ev = None
        bcrow_t_ev = None
        cint3 = cint.rearrange("q (t n) -> q t n", n=NS)
        for b in range(NB):
            built = None
            for grp in range(2 * NS // 4):
                bce = p.dma("sync", bcrow.rearrange("o (r t) -> o r t", t=TS),
                            self.bc_sp[grp * 4:(grp + 1) * 4, b * TS:(b + 1) * TS],
                            waits=[bc_free, bcrow_t_ev], key="bcr")
                for i in range(4):
                    n = grp * 4 + i
                    pbk = self.pb[n % 2]
                    me = p.do("tensor", lambda pbk=pbk, i=i: nc.tensor.matmul(
                        out=pbk[:, 0:TS], lhsT=self.ones_row,
                        rhs=bcrow[0:1, i * TS:(i + 1) * TS], start=True, stop=True),
                        waits=[bce] + self.pwaits(pbk), inc=True)
                    bcrow_t_ev = me
                    if n < NS:
                        ee = p.do("scalar", lambda pbk=pbk, n=n: nc.scalar.copy(
                            out=bbc[:, n * TS:(n + 1) * TS], in_=pbk[:, 0:TS]),
                            waits=[me] + ([cm_ev] if n == 0 else []), inc=True)
                    else:
                        nn = n - NS
                        ee = p.do("scalar", lambda pbk=pbk, nn=nn: nc.scalar.copy(
                            out=cint3[:, :, nn:nn + 1], in_=pbk[:, 0:TS]),
                            waits=[me], inc=True)
                    self.pset(pbk, ee)
            built = ee

            for j in range(DCH):
                r = j % 2
                lde = p.dma("sync", dt_t[r].rearrange("q (c t) -> q c t", t=T1),
                            dtsp[:, b * CPB:(b + 1) * CPB, j, :],
                            waits=[dt_free[r]], key=f"dtl{r}")
                lxe = p.dma("sync", xc_t[r].rearrange("q (c t) -> q c t", t=T1),
                            xcsp[:, b * CPB:(b + 1) * CPB, j, :],
                            waits=[xc_free[r]], key=f"xcl{r}")
                lze = p.dma("sync", sz_t[r].rearrange("q (c t) -> q c t", t=T1),
                            szsp[:, b * CPB:(b + 1) * CPB, j, :],
                            waits=[sz_free[r]], key=f"szl{r}")
                dxe = p.do("vector", lambda r=r: nc.vector.tensor_mul(
                    out=dtx, in0=dt_t[r], in1=xc_t[r]), waits=[lde, lxe, y3e],
                    inc=True)
                hs = h_int[j % 2]
                h3 = hs.rearrange("q (t n) -> q t n", n=NS)
                last_tts = None
                for n in range(NS):
                    r2 = n % 2
                    ae = p.do("scalar", lambda r=r, r2=r2, j=j, n=n:
                              nc.scalar.activation(
                        out=a_t[r2], in_=dt_t[r], func=AF.Exp,
                        scale=self.A_f[:, j * NS + n:j * NS + n + 1]),
                        waits=[lde, au_free[0][r2]], inc=True)
                    ue = p.do("vector", lambda r2=r2, n=n: nc.vector.tensor_mul(
                        out=u_t[r2], in0=dtx, in1=bbc[:, n * TS:(n + 1) * TS]),
                        waits=[built, dxe, au_free[1][r2]], inc=True)
                    hsl = hs.rearrange("q (t n) -> q n t", n=NS)[:, n:n + 1, :]\
                        .rearrange("q o t -> q (o t)")
                    last_tts = p.do("vector", lambda hsl=hsl, r2=r2, n=n, j=j:
                                    nc.vector.tensor_tensor_scan(
                        out=hsl, data0=a_t[r2], data1=u_t[r2],
                        initial=self.state[:, j * NS + n:j * NS + n + 1],
                        op0=ALU.mult, op1=ALU.add),
                        waits=[ae, h_free[j % 2]], inc=True)
                    au_free[0][r2] = last_tts
                    au_free[1][r2] = last_tts
                dt_free[r] = last_tts
                stev = p.do("scalar", lambda hs=hs, j=j: nc.scalar.copy(
                    out=self.state[:, j * NS:(j + 1) * NS],
                    in_=hs[:, (TS - 1) * NS:TS * NS]), waits=[last_tts], inc=True)
                cm_ev = p.do("vector", lambda hs=hs: nc.vector.tensor_mul(
                    out=hs, in0=hs, in1=cint), waits=[stev], inc=True)
                ye = p.do("vector", lambda h3=h3: nc.vector.tensor_reduce(
                    out=y_t, in_=h3, axis=mybir.AxisListType.X, op=ALU.add), inc=True)
                h_free[j % 2] = ye
                xc_free[r] = p.do("vector", lambda r=r, j=j:
                                  nc.vector.scalar_tensor_tensor(
                    out=y_t, in0=xc_t[r], scalar=self.dskip_f[:, j:j + 1], in1=y_t,
                    op0=ALU.mult, op1=ALU.add), inc=True)
                y3e = p.do("vector", lambda r=r: nc.vector.tensor_mul(
                    out=y3_t[r], in0=y_t, in1=sz_t[r]),
                    waits=[lze, y3s_ev[r]], inc=True)
                sz_free[r] = y3e
                spe = p.dma("gpsimd", y3sp[:, b * CPB:(b + 1) * CPB, j, :],
                            y3_t[r].rearrange("q (c t) -> q c t", t=T1),
                            waits=[y3e], key=f"y3s{r}")
                y3s_ev[r] = spe
            bc_free = cm_ev
            self._y3done[b] = [y3s_ev[0], y3s_ev[1]]
            yield b

    def phase3(self):
        nc, p = self.nc, self.p
        L, T3 = self.L, self.T3
        NCH = L // T3
        NSUB = T3 // 128
        P3S = self.cfg.get("p3stage", 9)
        y33 = self.tile("y33", [128, DCH * T3], BF)
        x_tm = self.tile("x_tm3", [128, NSUB * D], FR)
        x_fm = self.tile("x_fm3", [128, XCH * T3], FR)
        ha3 = self.tile("ha3", [128, XCH * T3], FR)
        xin2 = self.tile("xin2", [128, XCH * T3], FR)
        h2 = self.tile("h2", [128, XCH * T3], BF)
        x2 = self.tile("x2", [128, XCH * T3], FR)
        hb = self.tile("hb", [128, XCH * T3], BF)
        outf = self.tile("outf", [128, XCH * T3], F32)
        out_tm = self.tile("out_tm3", [128, NSUB * D], F32)
        sq = self.tile("sq3", [128, XCH * T3], FR)
        tmp = self.tile("tmp3", [128, T3], FR)
        rbc = self.tile("rbc3", [128, T3], FR)
        nbc = self.tile("nbc3", [128, T3], FR)
        wo_sl = [self.tile(f"wo{i}", [128, D], BF) for i in range(2)]
        wf1a_sl = [self.tile(f"wf1a{i}", [128, XCH * 128], BF) for i in range(2)]
        wf1g_sl = [self.tile(f"wf1g{i}", [128, XCH * 128], BF) for i in range(2)]
        wf2_sl = [self.tile(f"wf2_{i}", [128, D], BF) for i in range(2)]
        wc_sl = [self.tile(f"wc{i}", [128, D], BF) for i in range(2)]
        gel = self.tile("gel", [128, T3], FR)
        ag_full = self.tile("ag_full", [128, FCH2 * T3], BF)
        ftmp = self.tile("ftmp", [128, T3], FR)

        self.alloc_stats(T3)
        assert T3 == self.T1
        wf1r = self.wf1_sp.rearrange("a b -> (a b)").rearrange(
            "(c q km) -> c q km", q=128, km=XCH * 128)
        wcr = self.wc_sp.rearrange("(k q) m -> q k m", q=128)

        y3_free = None; x_free = None; ha_free = None
        sl_ev = {"wo": [None, None], "wf1a": [None, None], "wf1g": [None, None],
                 "wf2": [None, None], "wc": [None, None]}
        ag_free = None
        gel_ev = None
        out_done = None
        h2_free = None
        hb_free = None
        ln2_stats_prev = None
        lnb_stats_prev = None
        x2_apply_prev = None
        outf_free = None
        st_ev = [None] * NSUB
        for c in range(NCH):
            ye = p.dma("gpsimd", y33,
                       self.y3_sp[:, c * DCH * T3:(c + 1) * DCH * T3],
                       waits=[y3_free] + self._y3done[c * T3 // self.TS], key="y3l")
            if c == 0 and "y33_sp" in set(self.cfg.get("dump", ())):
                p.dma("gpsimd", self.y33_sp, y33, waits=[ye], key="dbgy")
            hae = p.dma("gpsimd", ha3,
                        self.ha_sp[:, c * XCH * T3:(c + 1) * XCH * T3],
                        waits=[ha_free], key="hal")
            lev = None
            for s in range(NSUB):
                lev = p.dma("gpsimd", x_tm[:, s * D:(s + 1) * D],
                            self.x_h[c * T3 + s * 128:c * T3 + (s + 1) * 128, :],
                            waits=[x_free], key="x3")
            x_ev = None
            for fc in range(XCH):
                tev = None
                for s in range(NSUB):
                    tev = p.do("tensor", lambda s=s, fc=fc: nc.tensor.transpose(
                        out=self.pbr[:, s * 128:(s + 1) * 128],
                        in_=x_tm[:, s * D + fc * 128:s * D + (fc + 1) * 128],
                        identity=self.eye_r),
                        waits=[lev] + (self.pwaits(self.pbr) if s == 0 else []),
                        inc=(s == NSUB - 1))
                x_ev = p.do("scalar", lambda fc=fc: nc.scalar.copy(
                    out=x_fm[:, fc * T3:(fc + 1) * T3], in_=self.pbr[:, 0:T3]),
                    waits=[tev], inc=True)
                self.pset(self.pbr, x_ev)
            x_free = x_ev

            if P3S <= 1:
                continue
            xin2e = None
            for half in range(2):
                mev = [None] * 4
                for kc in range(DCH):
                    r = kc % 2
                    we = p.dma("gpsimd", wo_sl[r][:, 0:512],
                               self.wo_sp[kc * 128:(kc + 1) * 128,
                                          half * 512:half * 512 + 512],
                               waits=[sl_ev["wo"][r]], key=f"wo{r}")
                    for m4 in range(4):
                        m = half * 4 + m4
                        ev = p.do("tensor", lambda kc=kc, m4=m4, r=r:
                                       nc.tensor.matmul(
                            out=self.pb[m4][:, 0:T3],
                            lhsT=wo_sl[r][:, m4 * 128:(m4 + 1) * 128],
                            rhs=y33[:, kc * T3:(kc + 1) * T3],
                            start=(kc == 0), stop=(kc == DCH - 1)),
                            waits=[we, ye] +
                                  (self.pwaits(self.pb[m4]) if kc == 0 else []),
                            inc=(m4 == 3 or kc == DCH - 1))
                        if kc == DCH - 1:
                            mev[m4] = ev
                    sl_ev["wo"][r] = ev
                for m4 in range(4):
                    m = half * 4 + m4
                    xin2e = p.do("vector", lambda m=m, m4=m4: nc.vector.tensor_add(
                        out=xin2[:, m * T3:(m + 1) * T3], in0=self.pb[m4][:, 0:T3],
                        in1=ha3[:, m * T3:(m + 1) * T3]),
                        waits=[mev[m4], hae, ln2_stats_prev], inc=True)
                    self.pset(self.pb[m4], xin2e)
            y3_free = mev[3]
            ha_free = xin2e
            dbg = "x2_sp" in set(self.cfg.get("dump", ()))
            if dbg:
                dde = p.dma("gpsimd", self.pre_sp.rearrange("q (j l) -> q j l", l=L)[
                    :, :, c * T3:(c + 1) * T3],
                    xin2.rearrange("q (j t) -> q j t", t=T3), waits=[xin2e],
                    key="dbg3")
            if P3S <= 2:
                continue
            g2 = lambda kc: self.ln2g_f[:, kc:kc + 1]
            b2 = lambda kc: self.ln2b_f[:, kc:kc + 1]
            h2e, ln2_stats_prev = self.layer_norm(
                T3, xin2, h2, g2, b2, xin2e, tmp, sq, rbc, nbc,
                banks=(0, 1, 2, 3), dst_free=[h2_free])
            if dbg:
                p.dma("gpsimd", self.h2_sp.rearrange("q (j l) -> q j l", l=L)[
                    :, :, c * T3:(c + 1) * T3],
                    h2.rearrange("q (j t) -> q j t", t=T3), waits=[h2e], key="dbg4")
            if P3S <= 3:
                continue
            age = None
            for kc2 in range(FCH2):
                r = kc2 % 2
                w1ge = p.dma("gpsimd", wf1g_sl[r],
                             wf1r[FCH2 + kc2],
                             waits=[sl_ev["wf1g"][r]], key=f"wg{r}")
                w1ae = p.dma("gpsimd", wf1a_sl[r],
                             wf1r[kc2],
                             waits=[sl_ev["wf1a"][r]], key=f"wa{r}")
                gb_ = self.pb[r]
                ab_ = self.pb[2 + r]
                ge = None
                for kc in range(XCH):
                    ge = p.do("tensor", lambda kc=kc, r=r, gb_=gb_: nc.tensor.matmul(
                        out=gb_[:, 0:T3], lhsT=wf1g_sl[r][:, kc * 128:(kc + 1) * 128],
                        rhs=h2[:, kc * T3:(kc + 1) * T3],
                        start=(kc == 0), stop=(kc == XCH - 1)),
                        waits=[w1ge, h2e] + (self.pwaits(gb_) if kc == 0 else []),
                        inc=(kc == XCH - 1))
                sl_ev["wf1g"][r] = ge
                gse = p.do("scalar", lambda kc2=kc2, gb_=gb_: nc.scalar.activation(
                    out=gel, in_=gb_[:, 0:T3], func=AF.Gelu_apprx_tanh,
                    bias=self.bff1_f[:, FCH2 + kc2:FCH2 + kc2 + 1]),
                    waits=[ge, gel_ev], inc=True)
                self.pset(gb_, gse)
                ae = None
                for kc in range(XCH):
                    ae = p.do("tensor", lambda kc=kc, r=r, ab_=ab_: nc.tensor.matmul(
                        out=ab_[:, 0:T3], lhsT=wf1a_sl[r][:, kc * 128:(kc + 1) * 128],
                        rhs=h2[:, kc * T3:(kc + 1) * T3],
                        start=(kc == 0), stop=(kc == XCH - 1)),
                        waits=[w1ae] + (self.pwaits(ab_) if kc == 0 else []),
                        inc=(kc == XCH - 1))
                sl_ev["wf1a"][r] = ae
                if kc2 == FCH2 - 1:
                    h2_free = ae
                age = p.do("vector", lambda kc2=kc2, ab_=ab_:
                           nc.vector.scalar_tensor_tensor(
                    out=ag_full[:, kc2 * T3:(kc2 + 1) * T3], in0=ab_[:, 0:T3],
                    scalar=self.bff1_f[:, kc2:kc2 + 1],
                    in1=gel, op0=ALU.add, op1=ALU.mult),
                    waits=[ae, gse, ag_free], inc=True)
                self.pset(ab_, age)
                gel_ev = age
            ag_done = age
            if P3S <= 4:
                continue
            x2e = None
            f2last = None
            for half in range(2):
                f2ev = [None] * 4
                for kc2 in range(FCH2):
                    r = kc2 % 2
                    w2e = p.dma("gpsimd", wf2_sl[r][:, 0:512],
                                self.wf2_sp[kc2 * 128:(kc2 + 1) * 128,
                                            half * 512:half * 512 + 512],
                                waits=[sl_ev["wf2"][r]], key=f"wf{r}")
                    for m4 in range(4):
                        ev = p.do("tensor", lambda kc2=kc2, m4=m4, r=r:
                                        nc.tensor.matmul(
                            out=self.pb[m4][:, 0:T3],
                            lhsT=wf2_sl[r][:, m4 * 128:(m4 + 1) * 128],
                            rhs=ag_full[:, kc2 * T3:(kc2 + 1) * T3],
                            start=(kc2 == 0), stop=(kc2 == FCH2 - 1)),
                            waits=[w2e, ag_done] +
                                  (self.pwaits(self.pb[m4]) if kc2 == 0 else []),
                            inc=(m4 == 3 or kc2 == FCH2 - 1))
                        if kc2 == FCH2 - 1:
                            f2ev[m4] = ev
                    sl_ev["wf2"][r] = ev
                f2last = f2ev[3]
                for m4 in range(4):
                    m = half * 4 + m4
                    fe = p.do("vector", lambda m=m, m4=m4: nc.vector.scalar_tensor_tensor(
                        out=xin2[:, m * T3:(m + 1) * T3], in0=self.pb[m4][:, 0:T3],
                        scalar=self.bff2_f[:, m:m + 1],
                        in1=xin2[:, m * T3:(m + 1) * T3],
                        op0=ALU.add, op1=ALU.add),
                        waits=[f2ev[m4]] + ([dde] if dbg else []), inc=True)
                    self.pset(self.pb[m4], fe)
                    x2e = p.do("vector", lambda m=m: nc.vector.scalar_tensor_tensor(
                        out=x2[:, m * T3:(m + 1) * T3],
                        in0=xin2[:, m * T3:(m + 1) * T3],
                        scalar=self.mod_f[:, 2 * XCH + m:2 * XCH + m + 1],
                        in1=x_fm[:, m * T3:(m + 1) * T3],
                        op0=ALU.mult, op1=ALU.add),
                        waits=[lnb_stats_prev], inc=(m == XCH - 1))
            ag_free = f2last
            cb = lambda kc: self.mod_f[:, 4 * XCH + kc:4 * XCH + kc + 1]
            sbv = lambda kc: self.mod_f[:, 3 * XCH + kc:3 * XCH + kc + 1]
            if "x2_sp" in set(self.cfg.get("dump", ())):
                p.dma("gpsimd", self.x2_sp.rearrange("q (j l) -> q j l", l=L)[
                    :, :, c * T3:(c + 1) * T3],
                    x2.rearrange("q (j t) -> q j t", t=T3), waits=[x2e], key="dbg1")
                p.dma("gpsimd", self.xin2_sp.rearrange("q (j l) -> q j l", l=L)[
                    :, :, c * T3:(c + 1) * T3],
                    xin2.rearrange("q (j t) -> q j t", t=T3), waits=[x2e], key="dbg2")
            if P3S <= 5:
                continue
            x2_apply_prev = x2e
            hbe, lnb_stats_prev = self.layer_norm(
                T3, x2, hb, cb, sbv, x2e, tmp, sq, rbc, nbc,
                banks=(0, 1, 2, 3), dst_free=[hb_free])
            if P3S <= 6:
                continue
            oute = None
            fte = None
            for half in range(2):
                fev = [None] * 4
                for kc in range(XCH):
                    r = kc % 2
                    wce = p.dma("gpsimd", wc_sl[r][:, 0:512],
                                self.wc_sp[kc * 128:(kc + 1) * 128,
                                           half * 512:half * 512 + 512],
                                waits=[sl_ev["wc"][r]], key=f"wc{r}")
                    for m4 in range(4):
                        ev = p.do("tensor", lambda kc=kc, m4=m4, r=r:
                                       nc.tensor.matmul(
                            out=self.pb[m4][:, 0:T3],
                            lhsT=wc_sl[r][:, m4 * 128:(m4 + 1) * 128],
                            rhs=hb[:, kc * T3:(kc + 1) * T3],
                            start=(kc == 0), stop=(kc == XCH - 1)),
                            waits=[wce, hbe] +
                                  (self.pwaits(self.pb[m4]) if kc == 0 else []),
                            inc=(m4 == 3 or kc == XCH - 1))
                        if kc == XCH - 1:
                            fev[m4] = ev
                    sl_ev["wc"][r] = ev
                hb_free = fev[3]
                for m4 in range(4):
                    m = half * 4 + m4
                    fte = p.do("scalar", lambda m=m, m4=m4: nc.scalar.activation(
                        out=ftmp, in_=self.pb[m4][:, 0:T3], func=AF.Tanh,
                        bias=self.fbias_f[:, m:m + 1]), waits=[fev[m4], oute],
                        inc=True)
                    self.pset(self.pb[m4], fte)
                    oute = p.do("vector", lambda m=m: nc.vector.scalar_tensor_tensor(
                        out=outf[:, m * T3:(m + 1) * T3], in0=ftmp,
                        scalar=self.mod_f[:, 5 * XCH + m:5 * XCH + m + 1],
                        in1=x2[:, m * T3:(m + 1) * T3], op0=ALU.mult, op1=ALU.add),
                        waits=[fte, outf_free], inc=True)
            if P3S <= 7:
                continue
            for s in range(NSUB):
                eev = None
                for fc4 in range(XCH // 4):
                    pbk = self.pb[fc4 % 2]
                    tev = None
                    for q in range(4):
                        fc = fc4 * 4 + q
                        tev = p.do("tensor", lambda pbk=pbk, s=s, fc=fc, q=q:
                                   nc.tensor.transpose(
                            out=pbk[:, q * 128:(q + 1) * 128],
                            in_=outf[:, fc * T3 + s * 128:fc * T3 + (s + 1) * 128],
                            identity=self.eye_f),
                            waits=[oute] + (self.pwaits(pbk) if q == 0 else []),
                            inc=(q == 3))
                    eev = p.do("vector", lambda pbk=pbk, s=s, fc4=fc4:
                               nc.vector.tensor_copy(
                        out=out_tm[:, s * D + fc4 * 512:s * D + (fc4 + 1) * 512],
                        in_=pbk[:, 0:512]), waits=[tev, st_ev[s]], inc=True)
                    self.pset(pbk, eev)
                    outf_free = tev
                out_done = p.dma("gpsimd",
                                 self.out_h[c * T3 + s * 128:c * T3 + (s + 1) * 128, :],
                                 out_tm[:, s * D:(s + 1) * D], waits=[eev, out_done],
                                 key="outs")
                st_ev[s] = out_done
            yield c
        p.wait("sync", out_done)
        yield -1


def build(cfg):
    mk = MK(cfg)
    return mk.build()


from concourse.bass_utils import run_bass_kernel_spmd

_CFG = dict(L=2048, T1=256, TS=256, T3=256, upto=3)
_EYE = np.eye(128, dtype=np.float32)


def kernel(**inputs):
    m = MK(dict(_CFG))
    nc = m.build()
    x = np.ascontiguousarray(inputs["L0_attn_x"], dtype=np.float32)
    c = np.ascontiguousarray(inputs["L0_c"], dtype=np.float32)
    shared = dict(
        ln1_g=inputs["L0_attn_ln1_g"], ln1_b=inputs["L0_attn_ln1_b"],
        w_in=inputs["L0_attn_w_in"], conv_w=inputs["L0_attn_conv_w"],
        conv_b=inputs["L0_attn_conv_b"], w_x=inputs["L0_attn_w_x"],
        w_dt=inputs["L0_attn_w_dt"], b_dt=inputs["L0_attn_b_dt"],
        a_log=inputs["L0_attn_a_log"], d_skip=inputs["L0_attn_d_skip"],
        w_out=inputs["L0_attn_w_out"], ln2_g=inputs["L0_attn_ln2_g"],
        ln2_b=inputs["L0_attn_ln2_b"], w_ff1=inputs["L0_attn_w_ff1"],
        b_ff1=inputs["L0_attn_b_ff1"], w_ff2=inputs["L0_attn_w_ff2"],
        b_ff2=inputs["L0_attn_b_ff2"], fw=inputs["L0_ffn_weight"],
        fa1=inputs["L0_ffn_aux_weight1"], fa2=inputs["L0_ffn_aux_weight2"],
        fbias=inputs["L0_ffn_bias"], wmod=inputs["L0_wmod"],
        eye_h=_EYE,
    )
    shared = {k: np.ascontiguousarray(v, dtype=np.float32) for k, v in shared.items()}
    in_maps = [dict(shared, x=x[i], c_vec=c[i]) for i in range(NCORES)]
    res = run_bass_kernel_spmd(nc, in_maps, core_ids=list(range(NCORES)))
    return np.stack([np.asarray(res.results[i]["out"], dtype=np.float32)
                     for i in range(NCORES)], axis=0)


# version 10
# speedup vs baseline: 1.1851x; 1.1851x over previous
import numpy as np
from contextlib import ExitStack

import concourse.bass as bass
import concourse.mybir as mybir

F32 = mybir.dt.float32
FR = mybir.dt.float32r
BF = mybir.dt.bfloat16
AF = mybir.ActivationFunctionType
ALU = mybir.AluOpType

D, DI, DTR, NS, FF = 1024, 2048, 64, 16, 4096
EPS = 1e-5
XCH, DCH = D // 128, DI // 128
FCH2 = FF // 128
NCORES = 8
ENGINES = ("sync", "vector", "scalar", "tensor", "gpsimd")
SBUF_BYTES = 211936


class Plan:
    def __init__(self, nc, sems, mksem):
        self.nc = nc
        self.streams = {e: [] for e in ENGINES}
        self.sems = dict(sems)
        self.mksem = mksem
        self.counts = {e: 0 for e in ENGINES}
        self.dma_counts = {}
        self.waited = {}

    def _dma_sem(self, key):
        if key not in self.sems:
            self.sems[key] = self.mksem(key)
            self.dma_counts[key] = 0
        return self.sems[key]

    def wait(self, eng, ev):
        if ev is None:
            return
        pe, val = ev
        if pe == eng or val <= 0 or self.waited.get((eng, pe), 0) >= val:
            return
        self.waited[(eng, pe)] = val
        sem = self.sems[pe]
        self.streams[eng].append(lambda E, sem=sem, val=val: E.wait_ge(sem, val))

    def do(self, eng, fn, waits=(), inc=False):
        for ev in waits:
            if ev:
                self.wait(eng, ev)
        if inc:
            self.counts[eng] += 1
            tgt = self.counts[eng]
            sem = self.sems[eng]

            def closure(E, fn=fn, sem=sem):
                fn().then_inc(sem, 1)
            self.streams[eng].append(closure)
            return (eng, tgt)
        self.streams[eng].append(lambda E, fn=fn: fn())
        return None

    def dma(self, eng, out, in_, waits=(), slow=False, key=None, tr=False):
        assert key is not None, "dma needs a stream key"
        sem = self._dma_sem(key)
        for ev in waits:
            if ev:
                self.wait(eng, ev)
        self.dma_counts[key] += 16
        tgt = self.dma_counts[key]

        def closure(E, out=out, in_=in_, sem=sem, slow=slow, tr=tr):
            if tr:
                E.dma_start_transpose(out=out, in_=in_).then_inc(sem, 16)
                return
            kw = {"allow_slow_non_contiguous": True} if slow else {}
            E.dma_start(out=out, in_=in_, **kw).then_inc(sem, 16)
        self.streams[eng].append(closure)
        return (key, tgt)

    def barrier(self):
        for key, cnt in list(self.dma_counts.items()):
            self.wait("sync", (key, cnt))
        targets = {}
        for e in ENGINES:
            self.counts[e] += 1
            targets[e] = self.counts[e]
            sem = self.sems[e]
            if e == "sync":
                self.streams[e].append(lambda E, sem=sem: E.sem_inc(sem, 1))
            else:
                self.streams[e].append(
                    lambda E, sem=sem: E.drain().then_inc(sem, 1))
        for e in ENGINES:
            for pe in ENGINES:
                if pe != e:
                    self.wait(e, (pe, targets[pe]))


class MK:
    def __init__(self, cfg):
        self.cfg = cfg
        self.L = cfg["L"]
        self.T1 = cfg["T1"]; self.TS = cfg["TS"]; self.T3 = cfg["T3"]
        self.upto = cfg.get("upto", 3)
        self.nc = bass.Bass("TRN2", target_bir_lowering=False, detect_race_conditions=False)

    def _sb_at(self, name, shape, dtype, off):
        nc = self.nc
        h = nc.virtual_sbuf_tensor(name, list(shape), dtype)
        nc._tensor(h.name, h.shape, h.dtype, type="SB")
        mloc = nc.lookup_mloc(h)
        mloc.allocated = True
        mloc.addr = self._arena_addr + off
        return h.ap()

    def persist(self, name, shape, dtype=F32):
        per = int(np.prod(shape[1:])) * mybir.dt.size(dtype)
        off = -(-self._persist_cur // 32) * 32
        self._persist_cur = off + per
        assert self._persist_cur <= self._persist_limit, f"persist overflow: {name}"
        return self._sb_at(name, shape, dtype, off)

    def tile(self, name, shape, dtype=FR):
        per = int(np.prod(shape[1:])) * mybir.dt.size(dtype)
        off = -(-self._phase_cur // 32) * 32
        self._phase_cur = off + per
        assert self._phase_cur <= SBUF_BYTES, f"sbuf overflow: {name} @{self._phase_cur}"
        return self._sb_at(name, shape, dtype, off)

    def new_phase(self):
        self._phase_cur = self._persist_limit

    def alloc_stats(self, T):
        self.stat_mu = self.tile("stat_mu", [1, T], FR)
        self.stat_ms = self.tile("stat_ms", [1, T], FR)
        self.stat_r = self.tile("stat_r", [1, T], FR)
        self.stat_nmr = self.tile("stat_nmr", [1, T], FR)
        self.stat_vk = self.tile("stat_vk", [1, T], FR)
        self.stat_tn = self.tile("stat_tn", [1, T], FR)

    def pidx(self, pbap):
        return 7 if pbap is self.pbr else self.pb.index(pbap)

    def pwaits(self, pbap):
        return list(self.pfree[self.pidx(pbap)])

    def pset(self, pbap, ev):
        self.pfree[self.pidx(pbap)] = list(ev) if isinstance(ev, list) else [ev]

    def build(self):
        nc, cfg = self.nc, self.cfg
        L = self.L

        dr = lambda n, sh, dt=FR, kind="ExternalInput": \
            nc.dram_tensor(n, sh, dt, kind=kind).ap()
        self.x_h = dr("x", [L, D])
        self.ln1_g = dr("ln1_g", [D], F32); self.ln1_b = dr("ln1_b", [D], F32)
        self.w_in = dr("w_in", [D, 2 * DI])
        self.conv_w = dr("conv_w", [DI, 4], F32); self.conv_b = dr("conv_b", [DI], F32)
        self.w_x = dr("w_x", [DI, DTR + 2 * NS])
        self.w_dt = dr("w_dt", [DTR, DI]); self.b_dt = dr("b_dt", [DI], F32)
        self.a_log = dr("a_log", [DI, NS], F32); self.d_skip = dr("d_skip", [DI], F32)
        self.w_out = dr("w_out", [DI, D])
        self.ln2_g = dr("ln2_g", [D], F32); self.ln2_b = dr("ln2_b", [D], F32)
        self.w_ff1 = dr("w_ff1", [D, 2 * FF]); self.b_ff1 = dr("b_ff1", [2 * FF], F32)
        self.w_ff2 = dr("w_ff2", [FF, D]); self.b_ff2 = dr("b_ff2", [D], F32)
        self.fw = dr("fw", [D, D]); self.fa1 = dr("fa1", [D, D]); self.fa2 = dr("fa2", [D, D])
        self.fbias = dr("fbias", [D], F32)
        self.c_vec = dr("c_vec", [D], F32)
        self.wmod = dr("wmod", [D, 6 * D])
        self.eye_h = dr("eye_h", [128, 128])
        self.out_h = dr("out", [L, D], F32, kind="ExternalOutput")

        dumps = set(cfg.get("dump", ()))
        def scr(n, sh, dt=FR):
            kind = "ExternalOutput" if n in dumps else "Internal"
            return nc.dram_tensor(n, sh, dt, kind=kind).ap()
        self.ha_sp = scr("ha_sp", [128, XCH * L])
        self.xc_sp = scr("xc_sp", [128, DCH * L])
        self.dt_sp = scr("dt_sp", [128, DCH * L])
        self.sz_sp = scr("sz_sp", [128, DCH * L])
        self.bc_sp = scr("bc_sp", [2 * NS, L])
        self.y3_sp = scr("y3_sp", [128, DCH * L], BF)
        self.wc_sp = scr("wc_sp", [D, D], BF)
        self.wo_sp = scr("wo_sp", [DI, D], BF)
        self.xin2_sp = scr("xin2_sp", [128, XCH * L])
        self.pre_sp = scr("pre_sp", [128, XCH * L])
        self.h2_sp = scr("h2_sp", [128, XCH * L], BF)
        self.ag_sp = scr("ag_sp", [128, FCH2 * L], BF)
        self.x2_sp = scr("x2_sp", [128, XCH * L])
        self.wf1_sp = scr("wf1_sp", [D, 2 * FF], BF)
        self.wi_sp = scr("wi_sp", [D, 2 * DI], BF)
        self.wf2_sp = scr("wf2_sp", [FF, D], BF)
        self.halo_sp = scr("halo_sp", [128, DCH * 4])
        self.zero_sp = scr("zero_sp", [128, DCH * NS])
        self.y33_sp = scr("y33_sp", [128, DCH * 256], BF)

        with ExitStack() as st:
            arena = st.enter_context(
                nc.sbuf_tensor("arena", [128, SBUF_BYTES], mybir.dt.uint8))
            self._arena_addr = nc.lookup_mloc(arena).addr
            self._persist_cur = 0
            self._persist_limit = 8 * 1024
            self._phase_cur = self._persist_limit

            self.pb = [st.enter_context(nc.psum_tensor(f"pb{i}", [128, 512], F32)).ap()
                       for i in range(7)]
            self.pbr = st.enter_context(nc.psum_tensor("pbr", [128, 512], FR)).ap()
            sems = {e: st.enter_context(nc.semaphore("s_" + e)) for e in ENGINES}
            mksem = lambda name: st.enter_context(nc.semaphore("d_" + name))
            self.p = Plan(nc, sems, mksem)
            self.pfree = [[] for _ in range(8)]

            self.alloc_persist()
            self.phase0()
            self.p.barrier()
            self.phase0b()
            self.p.barrier()
            if self.upto >= 1:
                self.phase1()
                self.p.barrier()
            if self.upto >= 2:
                self.phase2()
                self.p.barrier()
            if self.upto >= 3:
                self.phase3()
            self.p.barrier()

            blk = st.enter_context(nc.Block())
            for e in ENGINES:
                fns = self.p.streams[e]

                def runner(E, fns=fns):
                    for f in fns:
                        f(E)
                getattr(blk, e)(runner)
        return nc

    def alloc_persist(self):
        p = self.persist
        self.ln1g_f = p("ln1g_f", [128, XCH]); self.ln1b_f = p("ln1b_f", [128, XCH])
        self.ln2g_f = p("ln2g_f", [128, XCH]); self.ln2b_f = p("ln2b_f", [128, XCH])
        self.convw_f = p("convw_f", [128, DCH * 4]); self.convb_f = p("convb_f", [128, DCH])
        self.bdt_f = p("bdt_f", [128, DCH]); self.dskip_f = p("dskip_f", [128, DCH])
        self.A_f = p("A_f", [128, DCH * NS])
        self.bff1_f = p("bff1_f", [128, 2 * FCH2]); self.bff2_f = p("bff2_f", [128, XCH])
        self.fbias_f = p("fbias_f", [128, XCH])
        self.cvec_f = p("cvec_f", [128, XCH]); self.cvec_r = p("cvec_r", [128, XCH], FR)
        self.mod_f = p("mod_f", [128, 6 * XCH])
        self.eye_r = p("eye_r", [128, 128], FR)
        self.eye_f = p("eye_f", [128, 128], F32)
        self.ones_col = p("ones_col", [128, 1], FR)
        self.ones_row = p("ones_row", [1, 128], FR)
        self.ones_cf = p("ones_cf", [128, 1], F32)
        self.zero_f = p("zero_f", [128, DCH * NS], F32)
        self.ones_rf = p("ones_rf", [1, 128], F32)
        self.state = p("state", [128, DCH * NS], FR)
        self.halo = p("halo", [128, DCH * 4], FR)

    def layer_norm(self, T, v_ap, dst_ap, g_col, b_col, vready, tmp, sq, rbc, nbc,
                   banks=(4, 5, 2, 3), dst_free=()):
        nc, p = self.nc, self.p
        fr = getattr(self, "_lnfree", {})
        self._lnfree = fr
        pb_mu, pb_ms, pb_r, pb_n = (self.pb[i] for i in banks)
        sev = p.do("scalar", lambda: nc.scalar.activation(
            out=sq, in_=v_ap, func=AF.Square), waits=[vready, fr.get("sq")], inc=True)
        mu_ev = ms_ev = None
        for pb, rhs_ap, wev in ((pb_mu, v_ap, vready), (pb_ms, sq, sev)):
            ev = None
            for kc in range(XCH):
                rhs = rhs_ap[:, kc * T:(kc + 1) * T]
                ev = p.do("tensor", lambda pb=pb, rhs=rhs, kc=kc: nc.tensor.matmul(
                    out=pb[0:1, 0:T], lhsT=self.ones_col, rhs=rhs,
                    start=(kc == 0), stop=(kc == XCH - 1)),
                    waits=[wev] + (self.pwaits(pb) if kc == 0 else []),
                    inc=(kc == XCH - 1))
            if pb is pb_mu:
                mu_ev = ev
            else:
                ms_ev = ev
        fr["sq"] = ms_ev
        smu, sms = self.stat_mu[:, 0:T], self.stat_ms[:, 0:T]
        sr, snm = self.stat_r[:, 0:T], self.stat_nmr[:, 0:T]
        p.do("vector", lambda: nc.vector.tensor_scalar(
            out=smu, in0=pb_mu[0:1, 0:T], scalar1=1.0 / D, scalar2=None, op0=ALU.mult),
            waits=[mu_ev, fr.get("stat")])
        ev = p.do("vector", lambda: nc.vector.tensor_scalar(
            out=sms, in0=pb_ms[0:1, 0:T], scalar1=1.0 / D, scalar2=EPS,
            op0=ALU.mult, op1=ALU.add), waits=[ms_ev], inc=True)
        self.pset(pb_mu, ev); self.pset(pb_ms, ev)
        p.do("vector", lambda: nc.vector.tensor_tensor(
            out=sr, in0=smu, in1=smu, op=ALU.mult))
        sub_ev = p.do("vector", lambda: nc.vector.tensor_sub(
            out=sms, in0=sms, in1=sr), inc=True)
        vkeep = self.stat_vk[:, 0:T]
        sqe = p.do("scalar", lambda: nc.scalar.activation(
            out=vkeep, in_=sms, func=AF.Sqrt), waits=[sub_ev], inc=True)
        def _recip():
            with nc.allow_low_precision("f32r == fp32 precision"):
                return nc.vector.reciprocal(out=sr, in_=vkeep)
        p.do("vector", _recip, waits=[sqe])
        tnr = self.stat_tn[:, 0:T]
        for _ in range(2):
            p.do("vector", lambda: nc.vector.tensor_mul(out=tnr, in0=sr, in1=sr))
            p.do("vector", lambda: nc.vector.tensor_mul(out=tnr, in0=tnr, in1=sms))
            p.do("vector", lambda: nc.vector.tensor_scalar(
                out=tnr, in0=tnr, scalar1=-0.5, scalar2=1.5,
                op0=ALU.mult, op1=ALU.add))
            p.do("vector", lambda: nc.vector.tensor_mul(out=sr, in0=sr, in1=tnr))
        rev = p.do("vector", lambda: nc.vector.engine_nop(), inc=True)
        p.do("vector", lambda: nc.vector.tensor_tensor(
            out=snm, in0=smu, in1=sr, op=ALU.mult))
        nev = p.do("vector", lambda: nc.vector.tensor_scalar(
            out=snm, in0=snm, scalar1=-1.0, scalar2=None, op0=ALU.mult), inc=True)
        bev1 = p.do("tensor", lambda: nc.tensor.matmul(
            out=pb_r[:, 0:T], lhsT=self.ones_row, rhs=sr, start=True, stop=True),
            waits=[rev] + self.pwaits(pb_r), inc=True)
        bev2 = p.do("tensor", lambda: nc.tensor.matmul(
            out=pb_n[:, 0:T], lhsT=self.ones_row, rhs=snm, start=True, stop=True),
            waits=[nev] + self.pwaits(pb_n), inc=True)
        fr["stat"] = bev2
        ce1 = p.do("scalar", lambda: nc.scalar.copy(out=rbc, in_=pb_r[:, 0:T]),
                   waits=[bev1, fr.get("rbc")], inc=True)
        ce2 = p.do("scalar", lambda: nc.scalar.copy(out=nbc, in_=pb_n[:, 0:T]),
                   waits=[bev2, fr.get("rbc")], inc=True)
        self.pset(pb_r, ce1); self.pset(pb_n, ce2)
        last = None
        for kc in range(XCH):
            vv = v_ap[:, kc * T:(kc + 1) * T]
            dd = dst_ap[:, kc * T:(kc + 1) * T]
            p.do("vector", lambda vv=vv: nc.vector.tensor_mul(
                out=tmp, in0=vv, in1=rbc), waits=[ce1, ce2] + list(dst_free))
            p.do("vector", lambda: nc.vector.tensor_add(out=tmp, in0=tmp, in1=nbc))
            last = p.do("vector", lambda kc=kc, dd=dd: nc.vector.tensor_scalar(
                out=dd, in0=tmp, scalar1=g_col(kc), scalar2=b_col(kc),
                op0=ALU.mult, op1=ALU.add), inc=(kc == XCH - 1))
        fr["rbc"] = last
        return last, ms_ev

    def phase0(self):
        nc, p = self.nc, self.p
        self.new_phase()
        loads = [(self.ln1g_f, self.ln1_g, D), (self.ln1b_f, self.ln1_b, D),
                 (self.ln2g_f, self.ln2_g, D), (self.ln2b_f, self.ln2_b, D),
                 (self.convb_f, self.conv_b, DI), (self.bdt_f, self.b_dt, DI),
                 (self.dskip_f, self.d_skip, DI), (self.bff2_f, self.b_ff2, D),
                 (self.fbias_f, self.fbias, D), (self.bff1_f, self.b_ff1, 2 * FF),
                 (self.cvec_f, self.c_vec, D)]
        VCAP = 11264
        vrow = self.tile("vrow", [1, VCAP], F32)
        rounds, cur, acc = [], [], 0
        for item in loads:
            if acc + item[2] > VCAP:
                rounds.append(cur); cur, acc = [], 0
            cur.append(item); acc += item[2]
        rounds.append(cur)
        vrow_free = None
        for j in range(DCH):
            p.dma("sync", self.convw_f[:, j * 4:(j + 1) * 4],
                  self.conv_w[j * 128:(j + 1) * 128, :], key="sv")
            p.dma("sync", self.A_f[:, j * NS:(j + 1) * NS],
                  self.a_log[j * 128:(j + 1) * 128, :], key="sv")
        sv_done = p.dma("sync", self.eye_r, self.eye_h, key="sv")
        eye_ev = p.do("vector", lambda: nc.vector.tensor_copy(
            out=self.eye_f, in_=self.eye_r), waits=[sv_done], inc=True)
        pbv = self.pb[3]
        vec_ev = {}
        for rnd in rounds:
            off = 0
            ld_ev = None
            for dst, src, n in rnd:
                ld_ev = p.dma("sync", vrow[:, off:off + n], src[None, :],
                              waits=[vrow_free], key="sv")
                off += n
            off = 0
            tev = None
            for dst, src, n in rnd:
                n128 = n // 128
                for i in range(n128):
                    tev = p.do("tensor", lambda off=off, i=i: nc.tensor.transpose(
                        out=pbv[:, i:i + 1],
                        in_=vrow[0:1, off + i * 128:off + (i + 1) * 128],
                        identity=self.eye_f[0:1, 0:1]),
                        waits=[ld_ev, sv_done, eye_ev] +
                              (self.pwaits(pbv) if i == 0 else []),
                        inc=(i == n128 - 1))
                cev = p.do("vector", lambda dst=dst, n128=n128:
                           nc.vector.tensor_copy(
                    out=dst[:, 0:n128], in_=pbv[:, 0:n128]), waits=[tev], inc=True)
                self.pset(pbv, cev)
                vec_ev[id(dst)] = cev
                off += n
            vrow_free = tev
        p.do("vector", lambda: nc.vector.memset(self.ones_cf, 1.0))
        p.do("vector", lambda: nc.vector.memset(self.ones_rf, 1.0))
        p.do("vector", lambda: nc.vector.tensor_copy(out=self.ones_col, in_=self.ones_cf))
        p.do("vector", lambda: nc.vector.tensor_copy(out=self.ones_row, in_=self.ones_rf))
        p.do("vector", lambda: nc.vector.memset(self.zero_f, 0.0))
        p.do("vector", lambda: nc.vector.tensor_copy(
            out=self.halo, in_=self.zero_f[:, 0:DCH * 4]))
        p.do("vector", lambda: nc.vector.tensor_copy(
            out=self.state, in_=self.zero_f), inc=True)
        p.do("scalar", lambda: nc.scalar.activation(
            out=self.A_f, in_=self.A_f, func=AF.Exp), waits=[sv_done])
        p.do("scalar", lambda: nc.scalar.mul(out=self.A_f, in_=self.A_f, mul=-1.0),
             inc=True)
        sev = p.do("scalar", lambda: nc.scalar.activation(
            out=self.cvec_f, in_=self.cvec_f, func=AF.Silu),
            waits=[sv_done, vec_ev[id(self.cvec_f)]], inc=True)
        cbe = p.do("vector", lambda: nc.vector.tensor_copy(
            out=self.cvec_r, in_=self.cvec_f), waits=[sev], inc=True)

        mod_row = self.tile("mod_row", [1, 6 * D], F32)
        wm_sl = [self.tile(f"wm{i}", [128, 512], FR) for i in range(2)]
        slot_ev = [None, None]
        eev = None
        for g in range(12):
            pbk = self.pb[g % 2]
            col0 = g * 512
            mev = None
            for kc in range(XCH):
                sl = wm_sl[kc % 2]
                lev = p.dma("sync", sl,
                            self.wmod[kc * 128:(kc + 1) * 128, col0:col0 + 512],
                            waits=[slot_ev[kc % 2]], key=f"wm{kc % 2}")
                mev = p.do("tensor", lambda pbk=pbk, sl=sl, kc=kc: nc.tensor.matmul(
                    out=pbk[0:1, :], lhsT=self.cvec_r[:, kc:kc + 1], rhs=sl,
                    start=(kc == 0), stop=(kc == XCH - 1)),
                    waits=[lev, cbe] + (self.pwaits(pbk) if kc == 0 else []),
                    inc=True)
                slot_ev[kc % 2] = mev
            eev = p.do("vector", lambda pbk=pbk, col0=col0: nc.vector.tensor_copy(
                out=mod_row[:, col0:col0 + 512], in_=pbk[0:1, :]), waits=[mev], inc=True)
            self.pset(pbk, eev)
        rsgn = self.tile("rsgn", [1, 2 * D], F32)
        rabs = self.tile("rabs", [1, 2 * D], F32)
        rowv = lambda v: mod_row[:, v * D:(v + 1) * D]
        p.do("vector", lambda: nc.vector.tensor_scalar(
            out=rowv(1), in0=rowv(1), scalar1=1.0, scalar2=None, op0=ALU.add),
            waits=[eev])
        p.do("vector", lambda: nc.vector.tensor_scalar(
            out=rowv(4), in0=rowv(4), scalar1=1.0, scalar2=None, op0=ALU.add))
        for i, v in enumerate((2, 5)):
            p.do("vector", lambda i=i, v=v: nc.vector.tensor_scalar(
                out=rsgn[:, i * D:(i + 1) * D], in0=rowv(v), scalar1=0.0,
                scalar2=None, op0=ALU.is_ge))
            ab = p.do("scalar", lambda i=i, v=v: nc.scalar.activation(
                out=rabs[:, i * D:(i + 1) * D], in_=rowv(v), func=AF.Abs),
                waits=[eev], inc=True)
        sg_ev = p.do("vector", lambda: nc.vector.tensor_scalar(
            out=rsgn, in0=rsgn, scalar1=2.0, scalar2=-1.0, op0=ALU.mult,
            op1=ALU.add), waits=[ab], inc=True)
        p.do("vector", lambda: nc.vector.tensor_scalar_max(
            out=rabs, in0=rabs, scalar1=0.25))
        cl_ev = p.do("vector", lambda: nc.vector.tensor_mul(
            out=rsgn, in0=rsgn, in1=rabs), inc=True)
        for i, v in enumerate((2, 5)):
            p.do("vector", lambda i=i, v=v: nc.vector.tensor_copy(
                out=rowv(v), in_=rsgn[:, i * D:(i + 1) * D]))
        row_done = p.do("vector", lambda: nc.vector.engine_nop(), inc=True)
        pbm = self.pb[2]
        tevs = []
        for i in range(6 * XCH):
            tevs.append(p.do("tensor", lambda i=i: nc.tensor.transpose(
                out=pbm[:, i:i + 1], in_=mod_row[0:1, i * 128:(i + 1) * 128],
                identity=self.eye_f[0:1, 0:1]),
                waits=[eev, row_done] + (self.pwaits(pbm) if i == 0 else []),
                inc=(i == 6 * XCH - 1)))
        mfe = p.do("vector", lambda: nc.vector.tensor_copy(
            out=self.mod_f, in_=pbm[:, 0:6 * XCH]), waits=[tevs[-1]], inc=True)
        self.pset(pbm, mfe)

        NSL = 3
        wprep = [self.tile(f"wprep{i}", [128, FF], FR) for i in range(NSL)]
        wprep_b = [self.tile(f"wprepb{i}", [128, FF], BF) for i in range(NSL)]
        cast_ce = [None] * NSL
        cast_se = [None] * NSL
        slot_i = [0]

        def cast_weight(src, dst, K, M):
            for kc in range(-(-K // 128)):
                kk = min(128, K - kc * 128)
                for mh in range(-(-M // FF)):
                    mm = min(FF, M - mh * FF)
                    i = slot_i[0] % NSL
                    slot_i[0] += 1
                    lq = "sync" if i % 2 == 0 else "scalar"
                    lev = p.dma(lq, wprep[i][0:kk, 0:mm],
                                src[kc * 128:kc * 128 + kk,
                                    mh * FF:mh * FF + mm],
                                waits=[cast_ce[i]], key=f"cast{i % 2}")
                    ce = p.do("vector", lambda i=i, kk=kk, mm=mm:
                              nc.vector.tensor_copy(
                        out=wprep_b[i][0:kk, 0:mm], in_=wprep[i][0:kk, 0:mm]),
                        waits=[lev, cast_se[i]], inc=True)
                    sev = p.dma("gpsimd",
                                dst[kc * 128:kc * 128 + kk, mh * FF:mh * FF + mm],
                                wprep_b[i][0:kk, 0:mm], waits=[ce], key="castst")
                    cast_ce[i] = ce
                    cast_se[i] = sev
            return sev

        cast_weight(self.w_ff1, self.wf1_sp, D, 2 * FF)
        cast_weight(self.w_ff2, self.wf2_sp, FF, D)
        cast_weight(self.w_out, self.wo_sp, DI, D)
        cast_weight(self.w_in, self.wi_sp, D, 2 * DI)

    def phase0b(self):
        nc, p = self.nc, self.p
        self.new_phase()
        slab = self.tile("p0slab", [128, D], FR)
        a1t = self.tile("a1t", [128, XCH * D], FR)
        h_sb = self.tile("h_sb", [128, XCH * D], FR)
        wt_sb = self.tile("wt_sb", [128, XCH * D], FR)
        a2_sb = self.tile("a2_sb", [128, XCH * D], FR)
        wc_row = self.tile("wc_row", [128, D], BF)

        def transpose_in(src_h, dst_sb):
            ev = None
            for rc in range(XCH):
                lev = p.dma("sync", slab, src_h[rc * 128:(rc + 1) * 128, :],
                            waits=[ev], key="slab")
                for cc in range(XCH):
                    tev = p.do("tensor", lambda cc=cc: nc.tensor.transpose(
                        out=self.pbr[:, 0:128],
                        in_=slab[:, cc * 128:(cc + 1) * 128], identity=self.eye_r),
                        waits=[lev] + self.pwaits(self.pbr), inc=True)
                    ev = p.do("scalar", lambda rc=rc, cc=cc: nc.scalar.copy(
                        out=dst_sb[:, cc * D + rc * 128:cc * D + (rc + 1) * 128],
                        in_=self.pbr[:, 0:128]), waits=[tev], inc=True)
                    self.pset(self.pbr, ev)
            return ev

        ta1 = transpose_in(self.fa1, a1t)
        a2ev = None
        for kc in range(XCH):
            a2ev = p.dma("sync", a2_sb[:, kc * D:(kc + 1) * D],
                         self.fa2[kc * 128:(kc + 1) * 128, :], key="a2")
        for k1c in range(XCH):
            for jh in range(2):
                pbk = self.pb[jh]
                mev = None
                for k2c in range(XCH):
                    mev = p.do("tensor", lambda pbk=pbk, k1c=k1c, jh=jh, k2c=k2c:
                               nc.tensor.matmul(
                        out=pbk[:, :],
                        lhsT=a1t[:, k2c * D + k1c * 128:k2c * D + (k1c + 1) * 128],
                        rhs=a2_sb[:, k2c * D + jh * 512:k2c * D + jh * 512 + 512],
                        start=(k2c == 0), stop=(k2c == XCH - 1)),
                        waits=[a2ev, ta1] + (self.pwaits(pbk) if k2c == 0 else []),
                        inc=(k2c == XCH - 1))
                eev = p.do("scalar", lambda pbk=pbk, k1c=k1c, jh=jh: nc.scalar.copy(
                    out=h_sb[:, k1c * D + jh * 512:k1c * D + jh * 512 + 512],
                    in_=pbk[:, :]), waits=[mev], inc=True)
                self.pset(pbk, eev)
        h_ev = eev
        twt = transpose_in(self.fw, wt_sb)
        for ic in range(XCH):
            sev = None
            for jh in range(2):
                pbk = self.pb[jh]
                mev = None
                for k1c in range(XCH):
                    mev = p.do("tensor", lambda pbk=pbk, ic=ic, jh=jh, k1c=k1c:
                               nc.tensor.matmul(
                        out=pbk[:, :],
                        lhsT=wt_sb[:, k1c * D + ic * 128:k1c * D + (ic + 1) * 128],
                        rhs=h_sb[:, k1c * D + jh * 512:k1c * D + jh * 512 + 512],
                        start=(k1c == 0), stop=(k1c == XCH - 1)),
                        waits=[h_ev, twt] + (self.pwaits(pbk) if k1c == 0 else []),
                        inc=(k1c == XCH - 1))
                eev = p.do("vector", lambda pbk=pbk, jh=jh: nc.vector.tensor_copy(
                    out=wc_row[:, jh * 512:jh * 512 + 512], in_=pbk[:, :]),
                    waits=[mev] + ([sev] if jh == 0 else []), inc=True)
                self.pset(pbk, eev)
            sev = p.dma("gpsimd", self.wc_sp[ic * 128:(ic + 1) * 128, :], wc_row,
                        waits=[eev], key="wcst")
            p.wait("vector", sev)

    def phase1(self):
        nc, p = self.nc, self.p
        L, T1 = self.L, self.T1
        NCH = L // T1
        NSUB = T1 // 128
        self.new_phase()
        w_in_sb = self.tile("w_in_sb", [128, XCH * 2 * DI], BF)
        wx_sb = self.tile("wx_sb", [128, DCH * 96], FR)
        wdt_sb = self.tile("wdt_sb", [64, DI], FR)
        x_tm = self.tile("x_tm", [128, NSUB * D], FR)
        x_fm = self.tile("x_fm", [128, XCH * T1], FR)
        ha_fm = self.tile("ha_fm", [128, XCH * T1], FR)
        h1 = self.tile("h1", [128, XCH * T1], BF)
        sq = self.tile("sq1", [128, XCH * T1], FR)
        tmp = self.tile("tmp1", [128, T1], FR)
        rbc = self.tile("rbc1", [128, T1], FR)
        nbc = self.tile("nbc1", [128, T1], FR)
        xcp = [self.tile(f"xcp{i}", [128, T1 + 4], FR) for i in range(2)]
        cvt = [self.tile(f"cvt{i}", [128, T1], FR) for i in range(2)]
        xc_full = self.tile("xc_full", [128, DCH * T1], FR)
        sz_rot = [self.tile(f"szr{i}", [128, T1], FR) for i in range(2)]
        dt_rot = [self.tile(f"dtr{i}", [128, T1], FR) for i in range(2)]
        sp1 = self.tile("sp1", [128, T1], FR)
        sp2 = self.tile("sp2", [128, T1], FR)
        dtlo = self.tile("dtlo", [64, T1], FR)
        dbc32 = self.tile("dbc32", [32, T1], FR)

        self.alloc_stats(T1)
        ev_w = None
        for kc in range(XCH):
            ev_w = p.dma("sync", w_in_sb[:, kc * 2 * DI:(kc + 1) * 2 * DI],
                         self.wi_sp[kc * 128:(kc + 1) * 128, :], key="w1")
        ev_wx = None
        for kc in range(DCH):
            ev_wx = p.dma("sync", wx_sb[:, kc * 96:(kc + 1) * 96],
                          self.w_x[kc * 128:(kc + 1) * 128, :], key="wx")
        ev_wdt = p.dma("sync", wdt_sb, self.w_dt, key="wdt")

        xcsp = self.xc_sp.rearrange("q (c j t) -> q c j t", j=DCH, t=T1)
        dtsp = self.dt_sp.rearrange("q (c j t) -> q c j t", j=DCH, t=T1)
        szsp = self.sz_sp.rearrange("q (c j t) -> q c j t", j=DCH, t=T1)
        if "halo_sp" in set(self.cfg.get("dump", ())):
            p.dma("gpsimd", self.halo_sp, self.halo, key="dbgh")
            p.dma("gpsimd", self.zero_sp, self.zero_f, key="dbgh")

        x_free = None
        ha_sp_ev = None
        xcs_last = None
        xcs_last_prev = None
        cvt_ev = [None, None]
        xcp_ev = [None, None]
        sz_ev = [None, None]
        dt_ev = [None, None]
        bc_ev = None
        ln1_stats_prev = None
        win_last_prev = None
        wx_last_prev = None
        wdt_last_prev = None
        lna_stats = None
        lna_apply_prev = None
        for c in range(NCH):
            xcs_last_prev = xcs_last
            lev = None
            for s in range(NSUB):
                lev = p.dma("sync", x_tm[:, s * D:(s + 1) * D],
                            self.x_h[c * T1 + s * 128:c * T1 + (s + 1) * 128, :],
                            waits=[x_free], key="x")
            x_ev = None
            for fc in range(XCH):
                tev = None
                for s in range(NSUB):
                    tev = p.do("tensor", lambda s=s, fc=fc: nc.tensor.transpose(
                        out=self.pbr[:, s * 128:(s + 1) * 128],
                        in_=x_tm[:, s * D + fc * 128:s * D + (fc + 1) * 128],
                        identity=self.eye_r),
                        waits=[lev] + (self.pwaits(self.pbr) if s == 0 else []),
                        inc=(s == NSUB - 1))
                x_ev = p.do("scalar", lambda fc=fc: nc.scalar.copy(
                    out=x_fm[:, fc * T1:(fc + 1) * T1], in_=self.pbr[:, 0:T1]),
                    waits=[tev, lna_stats, lna_apply_prev], inc=True)
                self.pset(self.pbr, x_ev)
            x_free = x_ev
            ca = lambda kc: self.mod_f[:, 1 * XCH + kc:1 * XCH + kc + 1]
            sa = lambda kc: self.mod_f[:, 0 * XCH + kc:0 * XCH + kc + 1]
            lna_apply_prev_new = None
            ha_ev, lna_stats = self.layer_norm(
                T1, x_fm, ha_fm, ca, sa, x_ev, tmp, sq, rbc, nbc,
                dst_free=[ln1_stats_prev, ha_sp_ev])
            lna_apply_prev = ha_ev
            ha_sp_ev = p.dma("gpsimd", self.ha_sp[:, c * XCH * T1:(c + 1) * XCH * T1],
                             ha_fm, waits=[ha_ev, ha_sp_ev], key="has")
            g1 = lambda kc: self.ln1g_f[:, kc:kc + 1]
            b1 = lambda kc: self.ln1b_f[:, kc:kc + 1]
            h1_ev, ln1_stats_prev = self.layer_norm(
                T1, ha_fm, h1, g1, b1, ha_ev, tmp, sq, rbc, nbc,
                dst_free=[win_last_prev])

            for mc in range(2 * DCH):
                pbk = self.pb[mc % 2]
                mev = None
                for kc in range(XCH):
                    mev = p.do("tensor", lambda pbk=pbk, mc=mc, kc=kc: nc.tensor.matmul(
                        out=pbk[:, 0:T1],
                        lhsT=w_in_sb[:, kc * 2 * DI + mc * 128:
                                     kc * 2 * DI + (mc + 1) * 128],
                        rhs=h1[:, kc * T1:(kc + 1) * T1],
                        start=(kc == 0), stop=(kc == XCH - 1)),
                        waits=[h1_ev, ev_w] + (self.pwaits(pbk) if kc == 0 else []),
                        inc=(kc == XCH - 1))
                if mc == 2 * DCH - 1:
                    win_last_prev = mev
                r = mc % 2
                if mc < DCH:
                    xp = xcp[r]
                    if c == 0:
                        p.do("vector", lambda xp=xp: nc.vector.tensor_copy(
                            out=xp[:, 0:4], in_=self.zero_f[:, 0:4]),
                            waits=[xcp_ev[r]])
                    else:
                        p.do("vector", lambda xp=xp, mc=mc: nc.vector.tensor_copy(
                            out=xp[:, 0:3], in_=self.halo[:, mc * 4:mc * 4 + 3]),
                            waits=[xcp_ev[r]])
                    eev = p.do("vector", lambda xp=xp, pbk=pbk: nc.vector.tensor_copy(
                        out=xp[:, 3:3 + T1], in_=pbk[:, 0:T1]), waits=[mev], inc=True)
                    hev = p.do("vector", lambda pbk=pbk, mc=mc: nc.vector.tensor_copy(
                        out=self.halo[:, mc * 4:mc * 4 + 3],
                        in_=pbk[:, T1 - 3:T1]), inc=True)
                    self.pset(pbk, [eev, hev])
                    cv = cvt[r]
                    wcol = lambda k, mc=mc: self.convw_f[:, mc * 4 + k:mc * 4 + k + 1]
                    p.do("vector", lambda cv=cv, xp=xp, wcol=wcol:
                         nc.vector.tensor_scalar(
                             out=cv, in0=xp[:, 0:T1], scalar1=wcol(0), scalar2=None,
                             op0=ALU.mult), waits=[cvt_ev[r]])
                    for k in (1, 2, 3):
                        p.do("vector", lambda cv=cv, xp=xp, k=k, wcol=wcol:
                             nc.vector.scalar_tensor_tensor(
                                 out=cv, in0=xp[:, k:k + T1], scalar=wcol(k), in1=cv,
                                 op0=ALU.mult, op1=ALU.add))
                    ve = p.do("vector", lambda: nc.vector.engine_nop(), inc=True)
                    xcp_ev[r] = ve
                    se = p.do("scalar", lambda cv=cv, mc=mc: nc.scalar.activation(
                        out=xc_full[:, mc * T1:(mc + 1) * T1], in_=cv, func=AF.Silu,
                        bias=self.convb_f[:, mc:mc + 1]),
                        waits=[ve, wx_last_prev, xcs_last_prev], inc=True)
                    cvt_ev[r] = se
                    xcs_last = p.dma("gpsimd", xcsp[:, c, mc, :],
                                     xc_full[:, mc * T1:(mc + 1) * T1],
                                     waits=[se], key="xcs")
                else:
                    j = mc - DCH
                    se = p.do("scalar", lambda r=r, pbk=pbk: nc.scalar.activation(
                        out=sz_rot[r], in_=pbk[:, 0:T1], func=AF.Silu),
                        waits=[mev, sz_ev[r]], inc=True)
                    self.pset(pbk, se)
                    spe = p.dma("gpsimd", szsp[:, c, j, :], sz_rot[r],
                                waits=[se], key=f"szs{r}")
                    sz_ev[r] = spe
            pbk = self.pb[2]
            mev = None
            for kc in range(DCH):
                mev = p.do("tensor", lambda pbk=pbk, kc=kc: nc.tensor.matmul(
                    out=pbk[0:96, 0:T1], lhsT=wx_sb[:, kc * 96:(kc + 1) * 96],
                    rhs=xc_full[:, kc * T1:(kc + 1) * T1],
                    start=(kc == 0), stop=(kc == DCH - 1)),
                    waits=[cvt_ev[0], cvt_ev[1], ev_wx] +
                          (self.pwaits(pbk) if kc == 0 else []),
                    inc=(kc == DCH - 1))
            wx_last_prev = mev
            e1 = p.do("vector", lambda pbk=pbk: nc.vector.tensor_copy(
                out=dtlo, in_=pbk[0:64, 0:T1]), waits=[mev, wdt_last_prev], inc=True)
            e2 = p.do("vector", lambda pbk=pbk: nc.vector.tensor_copy(
                out=dbc32, in_=pbk[64:96, 0:T1]), waits=[bc_ev], inc=True)
            self.pset(pbk, e2)
            bc_ev = p.dma("gpsimd", self.bc_sp[:, c * T1:(c + 1) * T1], dbc32,
                          waits=[e2], key="bcs")
            for mc in range(DCH):
                pbk = self.pb[3 + mc % 2]
                r = mc % 2
                mev = p.do("tensor", lambda pbk=pbk, mc=mc: nc.tensor.matmul(
                    out=pbk[:, 0:T1], lhsT=wdt_sb[:, mc * 128:(mc + 1) * 128],
                    rhs=dtlo, start=True, stop=True),
                    waits=[e1, ev_wdt] + self.pwaits(pbk), inc=True)
                if mc == DCH - 1:
                    wdt_last_prev = mev
                p.do("vector", lambda pbk=pbk, mc=mc: nc.vector.tensor_scalar(
                    out=sp1, in0=pbk[:, 0:T1], scalar1=self.bdt_f[:, mc:mc + 1],
                    scalar2=None, op0=ALU.add), waits=[mev, dt_ev[r]])
                ee = p.do("vector", lambda: nc.vector.engine_nop(), inc=True)
                self.pset(pbk, ee)
                p.do("scalar", lambda: nc.scalar.activation(
                    out=sp2, in_=sp1, func=AF.Abs), waits=[ee])
                p.do("scalar", lambda: nc.scalar.activation(
                    out=sp2, in_=sp2, func=AF.Exp, scale=-1.0))
                se = p.do("scalar", lambda: nc.scalar.activation(
                    out=sp2, in_=sp2, func=AF.Ln, bias=1.0), inc=True)
                p.do("vector", lambda: nc.vector.tensor_scalar_max(
                    out=sp1, in0=sp1, scalar1=0.0), waits=[se])
                de = p.do("vector", lambda r=r: nc.vector.tensor_add(
                    out=dt_rot[r], in0=sp1, in1=sp2), inc=True)
                spe = p.dma("gpsimd", dtsp[:, c, mc, :], dt_rot[r],
                            waits=[de], key=f"dts{r}")
                dt_ev[r] = spe

    def phase2(self):
        nc, p = self.nc, self.p
        L, TS = self.L, self.TS
        NB = L // TS
        self.new_phase()
        dt_t = [self.tile(f"p2dt{i}", [128, TS], FR) for i in range(2)]
        xc_t = [self.tile(f"p2xc{i}", [128, TS], FR) for i in range(2)]
        sz_t = [self.tile(f"p2sz{i}", [128, TS], FR) for i in range(2)]
        dtx = self.tile("p2dtx", [128, TS], FR)
        a_t = [self.tile(f"p2a{i}", [128, TS], FR) for i in range(2)]
        u_t = [self.tile(f"p2u{i}", [128, TS], FR) for i in range(2)]
        bcrow = self.tile("p2bcrow", [1, 4 * TS], FR)
        bbc = self.tile("p2bbc", [128, NS * TS], FR)
        cint = self.tile("p2cint", [128, NS * TS], FR)
        h_int = [self.tile(f"p2h{i}", [128, NS * TS], FR) for i in range(2)]
        y_t = self.tile("p2y", [128, TS], F32)
        y3_t = [self.tile(f"p2y3{i}", [128, TS], BF) for i in range(2)]
        self.alloc_stats(TS)

        T1 = self.T1
        assert TS == 2 * T1
        xcsp = self.xc_sp.rearrange("q (c j t) -> q c j t", j=DCH, t=T1)
        dtsp = self.dt_sp.rearrange("q (c j t) -> q c j t", j=DCH, t=T1)
        szsp = self.sz_sp.rearrange("q (c j t) -> q c j t", j=DCH, t=T1)
        y3sp = self.y3_sp.rearrange("q (c j t) -> q c j t", j=DCH, t=T1)

        dt_free = [None, None]; xc_free = [None, None]; sz_free = [None, None]
        au_free = [[None, None], [None, None]]
        h_free = [None, None]
        y3s_ev = [None, None]
        bc_free = None
        y3e = None
        cm_ev = None
        bcrow_t_ev = None
        cint3 = cint.rearrange("q (t n) -> q t n", n=NS)
        for b in range(NB):
            built = None
            for grp in range(2 * NS // 4):
                bce = p.dma("sync", bcrow.rearrange("o (r t) -> o r t", t=TS),
                            self.bc_sp[grp * 4:(grp + 1) * 4, b * TS:(b + 1) * TS],
                            waits=[bc_free, bcrow_t_ev], key="bcr")
                for i in range(4):
                    n = grp * 4 + i
                    pbk = self.pb[n % 2]
                    me = p.do("tensor", lambda pbk=pbk, i=i: nc.tensor.matmul(
                        out=pbk[:, 0:TS], lhsT=self.ones_row,
                        rhs=bcrow[0:1, i * TS:(i + 1) * TS], start=True, stop=True),
                        waits=[bce] + self.pwaits(pbk), inc=True)
                    bcrow_t_ev = me
                    if n < NS:
                        ee = p.do("scalar", lambda pbk=pbk, n=n: nc.scalar.copy(
                            out=bbc[:, n * TS:(n + 1) * TS], in_=pbk[:, 0:TS]),
                            waits=[me] + ([cm_ev] if n == 0 else []), inc=True)
                    else:
                        nn = n - NS
                        ee = p.do("scalar", lambda pbk=pbk, nn=nn: nc.scalar.copy(
                            out=cint3[:, :, nn:nn + 1], in_=pbk[:, 0:TS]),
                            waits=[me], inc=True)
                    self.pset(pbk, ee)
            built = ee

            for j in range(DCH):
                r = j % 2
                lde = p.dma("sync", dt_t[r].rearrange("q (c t) -> q c t", t=T1),
                            dtsp[:, 2 * b:2 * b + 2, j, :],
                            waits=[dt_free[r]], key=f"dtl{r}")
                lxe = p.dma("sync", xc_t[r].rearrange("q (c t) -> q c t", t=T1),
                            xcsp[:, 2 * b:2 * b + 2, j, :],
                            waits=[xc_free[r]], key=f"xcl{r}")
                lze = p.dma("sync", sz_t[r].rearrange("q (c t) -> q c t", t=T1),
                            szsp[:, 2 * b:2 * b + 2, j, :],
                            waits=[sz_free[r]], key=f"szl{r}")
                dxe = p.do("vector", lambda r=r: nc.vector.tensor_mul(
                    out=dtx, in0=dt_t[r], in1=xc_t[r]), waits=[lde, lxe, y3e],
                    inc=True)
                hs = h_int[j % 2]
                h3 = hs.rearrange("q (t n) -> q t n", n=NS)
                last_tts = None
                for n in range(NS):
                    r2 = n % 2
                    ae = p.do("scalar", lambda r=r, r2=r2, j=j, n=n:
                              nc.scalar.activation(
                        out=a_t[r2], in_=dt_t[r], func=AF.Exp,
                        scale=self.A_f[:, j * NS + n:j * NS + n + 1]),
                        waits=[lde, au_free[0][r2]], inc=True)
                    ue = p.do("vector", lambda r2=r2, n=n: nc.vector.tensor_mul(
                        out=u_t[r2], in0=dtx, in1=bbc[:, n * TS:(n + 1) * TS]),
                        waits=[built, dxe, au_free[1][r2]], inc=True)
                    hsl = hs.rearrange("q (t n) -> q n t", n=NS)[:, n:n + 1, :]\
                        .rearrange("q o t -> q (o t)")
                    last_tts = p.do("vector", lambda hsl=hsl, r2=r2, n=n, j=j:
                                    nc.vector.tensor_tensor_scan(
                        out=hsl, data0=a_t[r2], data1=u_t[r2],
                        initial=self.state[:, j * NS + n:j * NS + n + 1],
                        op0=ALU.mult, op1=ALU.add),
                        waits=[ae, h_free[j % 2]], inc=True)
                    au_free[0][r2] = last_tts
                    au_free[1][r2] = last_tts
                dt_free[r] = last_tts
                stev = p.do("scalar", lambda hs=hs, j=j: nc.scalar.copy(
                    out=self.state[:, j * NS:(j + 1) * NS],
                    in_=hs[:, (TS - 1) * NS:TS * NS]), waits=[last_tts], inc=True)
                cm_ev = p.do("vector", lambda hs=hs: nc.vector.tensor_mul(
                    out=hs, in0=hs, in1=cint), waits=[stev], inc=True)
                ye = p.do("vector", lambda h3=h3: nc.vector.tensor_reduce(
                    out=y_t, in_=h3, axis=mybir.AxisListType.X, op=ALU.add), inc=True)
                h_free[j % 2] = ye
                xc_free[r] = p.do("vector", lambda r=r, j=j:
                                  nc.vector.scalar_tensor_tensor(
                    out=y_t, in0=xc_t[r], scalar=self.dskip_f[:, j:j + 1], in1=y_t,
                    op0=ALU.mult, op1=ALU.add), inc=True)
                y3e = p.do("vector", lambda r=r: nc.vector.tensor_mul(
                    out=y3_t[r], in0=y_t, in1=sz_t[r]),
                    waits=[lze, y3s_ev[r]], inc=True)
                sz_free[r] = y3e
                spe = p.dma("gpsimd", y3sp[:, 2 * b:2 * b + 2, j, :],
                            y3_t[r].rearrange("q (c t) -> q c t", t=T1),
                            waits=[y3e], key=f"y3s{r}")
                y3s_ev[r] = spe
            bc_free = cm_ev

    def phase3(self):
        nc, p = self.nc, self.p
        L, T3 = self.L, self.T3
        NCH = L // T3
        NSUB = T3 // 128
        P3S = self.cfg.get("p3stage", 9)
        self.new_phase()
        y33 = self.tile("y33", [128, DCH * T3], BF)
        x_tm = self.tile("x_tm3", [128, NSUB * D], FR)
        x_fm = self.tile("x_fm3", [128, XCH * T3], FR)
        ha3 = self.tile("ha3", [128, XCH * T3], FR)
        xin2 = self.tile("xin2", [128, XCH * T3], FR)
        h2 = self.tile("h2", [128, XCH * T3], BF)
        x2 = self.tile("x2", [128, XCH * T3], FR)
        hb = self.tile("hb", [128, XCH * T3], BF)
        outf = self.tile("outf", [128, XCH * T3], F32)
        out_tm = self.tile("out_tm3", [128, NSUB * D], F32)
        sq = self.tile("sq3", [128, XCH * T3], FR)
        tmp = self.tile("tmp3", [128, T3], FR)
        rbc = self.tile("rbc3", [128, T3], FR)
        nbc = self.tile("nbc3", [128, T3], FR)
        wo_hl = [self.tile(f"wo{i}", [128, DCH * 512], BF) for i in range(2)]
        wf1a_sl = [self.tile(f"wf1a{i}", [128, XCH * 128], BF) for i in range(2)]
        wf1g_sl = [self.tile(f"wf1g{i}", [128, XCH * 128], BF) for i in range(2)]
        wf2_sl = [self.tile(f"wf2_{i}", [128, D], BF) for i in range(2)]
        wc_sl = [self.tile(f"wc{i}", [128, D], BF) for i in range(2)]
        gel = self.tile("gel", [128, T3], FR)
        ag_full = self.tile("ag_full", [128, FCH2 * T3], BF)
        ftmp = self.tile("ftmp", [128, T3], FR)

        self.alloc_stats(T3)
        assert T3 == self.T1
        wf1r = self.wf1_sp.rearrange("(k q) m -> q k m", q=128)
        wcr = self.wc_sp.rearrange("(k q) m -> q k m", q=128)

        y3_free = None; x_free = None; ha_free = None
        sl_ev = {"wo": [None, None], "wf1a": [None, None], "wf1g": [None, None],
                 "wf2": [None, None], "wc": [None, None]}
        ag_free = None
        gel_ev = None
        out_done = None
        h2_free = None
        hb_free = None
        ln2_stats_prev = None
        lnb_stats_prev = None
        x2_apply_prev = None
        outf_free = None
        st_ev = [None] * NSUB
        for c in range(NCH):
            ye = p.dma("sync", y33,
                       self.y3_sp[:, c * DCH * T3:(c + 1) * DCH * T3],
                       waits=[y3_free], key="y3l")
            if c == 0 and "y33_sp" in set(self.cfg.get("dump", ())):
                p.dma("gpsimd", self.y33_sp, y33, waits=[ye], key="dbgy")
            hae = p.dma("sync", ha3,
                        self.ha_sp[:, c * XCH * T3:(c + 1) * XCH * T3],
                        waits=[ha_free], key="hal")
            lev = None
            for s in range(NSUB):
                lev = p.dma("sync", x_tm[:, s * D:(s + 1) * D],
                            self.x_h[c * T3 + s * 128:c * T3 + (s + 1) * 128, :],
                            waits=[x_free], key="x3")
            x_ev = None
            for fc in range(XCH):
                tev = None
                for s in range(NSUB):
                    tev = p.do("tensor", lambda s=s, fc=fc: nc.tensor.transpose(
                        out=self.pbr[:, s * 128:(s + 1) * 128],
                        in_=x_tm[:, s * D + fc * 128:s * D + (fc + 1) * 128],
                        identity=self.eye_r),
                        waits=[lev] + (self.pwaits(self.pbr) if s == 0 else []),
                        inc=(s == NSUB - 1))
                x_ev = p.do("scalar", lambda fc=fc: nc.scalar.copy(
                    out=x_fm[:, fc * T3:(fc + 1) * T3], in_=self.pbr[:, 0:T3]),
                    waits=[tev], inc=True)
                self.pset(self.pbr, x_ev)
            x_free = x_ev

            if P3S <= 1:
                continue
            xin2e = None
            mev = [None] * 4
            for half in range(2):
                r = half
                we = p.dma("sync", wo_hl[r].rearrange("q (k m) -> q k m", m=512),
                           self.wo_sp.rearrange("(k q) m -> q k m", q=128)[
                               :, :, half * 512:half * 512 + 512],
                           waits=[sl_ev["wo"][r]], key=f"wo{r}")
                for m4 in range(4):
                    m = half * 4 + m4
                    for kc in range(DCH):
                        ev = p.do("tensor", lambda kc=kc, m4=m4, r=r:
                                       nc.tensor.matmul(
                            out=self.pb[m4][:, 0:T3],
                            lhsT=wo_hl[r][:, kc * 512 + m4 * 128:
                                          kc * 512 + (m4 + 1) * 128],
                            rhs=y33[:, kc * T3:(kc + 1) * T3],
                            start=(kc == 0), stop=(kc == DCH - 1)),
                            waits=[we, ye] +
                                  (self.pwaits(self.pb[m4]) if kc == 0 else []),
                            inc=(kc == DCH - 1))
                    mev[m4] = ev
                sl_ev["wo"][r] = ev
                for m4 in range(4):
                    m = half * 4 + m4
                    xin2e = p.do("vector", lambda m=m, m4=m4: nc.vector.tensor_add(
                        out=xin2[:, m * T3:(m + 1) * T3], in0=self.pb[m4][:, 0:T3],
                        in1=ha3[:, m * T3:(m + 1) * T3]),
                        waits=[mev[m4], hae, ln2_stats_prev], inc=True)
                    self.pset(self.pb[m4], xin2e)
            y3_free = mev[3]
            ha_free = xin2e
            dbg = "x2_sp" in set(self.cfg.get("dump", ()))
            if dbg:
                dde = p.dma("gpsimd", self.pre_sp.rearrange("q (j l) -> q j l", l=L)[
                    :, :, c * T3:(c + 1) * T3],
                    xin2.rearrange("q (j t) -> q j t", t=T3), waits=[xin2e],
                    key="dbg3")
            if P3S <= 2:
                continue
            g2 = lambda kc: self.ln2g_f[:, kc:kc + 1]
            b2 = lambda kc: self.ln2b_f[:, kc:kc + 1]
            h2e, ln2_stats_prev = self.layer_norm(
                T3, xin2, h2, g2, b2, xin2e, tmp, sq, rbc, nbc,
                banks=(0, 1, 2, 3), dst_free=[h2_free])
            if dbg:
                p.dma("gpsimd", self.h2_sp.rearrange("q (j l) -> q j l", l=L)[
                    :, :, c * T3:(c + 1) * T3],
                    h2.rearrange("q (j t) -> q j t", t=T3), waits=[h2e], key="dbg4")
            if P3S <= 3:
                continue
            age = None
            for kc2 in range(FCH2):
                r = kc2 % 2
                w1ge = p.dma("sync", wf1g_sl[r].rearrange("q (k m) -> q k m", m=128),
                             wf1r[:, :, (FCH2 + kc2) * 128:(FCH2 + kc2 + 1) * 128],
                             waits=[sl_ev["wf1g"][r]], key=f"wg{r}")
                w1ae = p.dma("sync", wf1a_sl[r].rearrange("q (k m) -> q k m", m=128),
                             wf1r[:, :, kc2 * 128:(kc2 + 1) * 128],
                             waits=[sl_ev["wf1a"][r]], key=f"wa{r}")
                gb_ = self.pb[r]
                ab_ = self.pb[2 + r]
                ge = None
                for kc in range(XCH):
                    ge = p.do("tensor", lambda kc=kc, r=r, gb_=gb_: nc.tensor.matmul(
                        out=gb_[:, 0:T3], lhsT=wf1g_sl[r][:, kc * 128:(kc + 1) * 128],
                        rhs=h2[:, kc * T3:(kc + 1) * T3],
                        start=(kc == 0), stop=(kc == XCH - 1)),
                        waits=[w1ge, h2e] + (self.pwaits(gb_) if kc == 0 else []),
                        inc=(kc == XCH - 1))
                sl_ev["wf1g"][r] = ge
                gse = p.do("scalar", lambda kc2=kc2, gb_=gb_: nc.scalar.activation(
                    out=gel, in_=gb_[:, 0:T3], func=AF.Gelu_apprx_tanh,
                    bias=self.bff1_f[:, FCH2 + kc2:FCH2 + kc2 + 1]),
                    waits=[ge, gel_ev], inc=True)
                self.pset(gb_, gse)
                ae = None
                for kc in range(XCH):
                    ae = p.do("tensor", lambda kc=kc, r=r, ab_=ab_: nc.tensor.matmul(
                        out=ab_[:, 0:T3], lhsT=wf1a_sl[r][:, kc * 128:(kc + 1) * 128],
                        rhs=h2[:, kc * T3:(kc + 1) * T3],
                        start=(kc == 0), stop=(kc == XCH - 1)),
                        waits=[w1ae] + (self.pwaits(ab_) if kc == 0 else []),
                        inc=(kc == XCH - 1))
                sl_ev["wf1a"][r] = ae
                if kc2 == FCH2 - 1:
                    h2_free = ae
                age = p.do("vector", lambda kc2=kc2, ab_=ab_:
                           nc.vector.scalar_tensor_tensor(
                    out=ag_full[:, kc2 * T3:(kc2 + 1) * T3], in0=ab_[:, 0:T3],
                    scalar=self.bff1_f[:, kc2:kc2 + 1],
                    in1=gel, op0=ALU.add, op1=ALU.mult),
                    waits=[ae, gse, ag_free], inc=True)
                self.pset(ab_, age)
                gel_ev = age
            ag_done = age
            if dbg:
                agde = p.dma("gpsimd", self.ag_sp.rearrange("q (j l) -> q j l", l=L)[
                    :, :, c * T3:(c + 1) * T3],
                    ag_full.rearrange("q (j t) -> q j t", t=T3), waits=[age],
                    key="dbg5")
            if P3S <= 4:
                continue
            x2e = None
            f2last = None
            for half in range(2):
                f2ev = [None] * 4
                for kc2 in range(FCH2):
                    r = kc2 % 2
                    w2e = p.dma("sync", wf2_sl[r][:, 0:512],
                                self.wf2_sp[kc2 * 128:(kc2 + 1) * 128,
                                            half * 512:half * 512 + 512],
                                waits=[sl_ev["wf2"][r]], key=f"wf{r}")
                    for m4 in range(4):
                        ev = p.do("tensor", lambda kc2=kc2, m4=m4, r=r:
                                        nc.tensor.matmul(
                            out=self.pb[m4][:, 0:T3],
                            lhsT=wf2_sl[r][:, m4 * 128:(m4 + 1) * 128],
                            rhs=ag_full[:, kc2 * T3:(kc2 + 1) * T3],
                            start=(kc2 == 0), stop=(kc2 == FCH2 - 1)),
                            waits=[w2e, ag_done] +
                                  (self.pwaits(self.pb[m4]) if kc2 == 0 else []),
                            inc=(m4 == 3 or kc2 == FCH2 - 1))
                        if kc2 == FCH2 - 1:
                            f2ev[m4] = ev
                    sl_ev["wf2"][r] = ev
                f2last = f2ev[3]
                for m4 in range(4):
                    m = half * 4 + m4
                    fe = p.do("vector", lambda m=m, m4=m4: nc.vector.scalar_tensor_tensor(
                        out=xin2[:, m * T3:(m + 1) * T3], in0=self.pb[m4][:, 0:T3],
                        scalar=self.bff2_f[:, m:m + 1],
                        in1=xin2[:, m * T3:(m + 1) * T3],
                        op0=ALU.add, op1=ALU.add),
                        waits=[f2ev[m4]] + ([dde] if dbg else []), inc=True)
                    self.pset(self.pb[m4], fe)
                    x2e = p.do("vector", lambda m=m: nc.vector.scalar_tensor_tensor(
                        out=x2[:, m * T3:(m + 1) * T3],
                        in0=xin2[:, m * T3:(m + 1) * T3],
                        scalar=self.mod_f[:, 2 * XCH + m:2 * XCH + m + 1],
                        in1=x_fm[:, m * T3:(m + 1) * T3],
                        op0=ALU.mult, op1=ALU.add),
                        waits=[lnb_stats_prev], inc=(m == XCH - 1))
            ag_free = f2last
            cb = lambda kc: self.mod_f[:, 4 * XCH + kc:4 * XCH + kc + 1]
            sbv = lambda kc: self.mod_f[:, 3 * XCH + kc:3 * XCH + kc + 1]
            if "x2_sp" in set(self.cfg.get("dump", ())):
                p.dma("gpsimd", self.x2_sp.rearrange("q (j l) -> q j l", l=L)[
                    :, :, c * T3:(c + 1) * T3],
                    x2.rearrange("q (j t) -> q j t", t=T3), waits=[x2e], key="dbg1")
                p.dma("gpsimd", self.xin2_sp.rearrange("q (j l) -> q j l", l=L)[
                    :, :, c * T3:(c + 1) * T3],
                    xin2.rearrange("q (j t) -> q j t", t=T3), waits=[x2e], key="dbg2")
            if P3S <= 5:
                continue
            x2_apply_prev = x2e
            hbe, lnb_stats_prev = self.layer_norm(
                T3, x2, hb, cb, sbv, x2e, tmp, sq, rbc, nbc,
                banks=(0, 1, 2, 3), dst_free=[hb_free])
            if P3S <= 6:
                continue
            oute = None
            fte = None
            for half in range(2):
                fev = [None] * 4
                for kc in range(XCH):
                    r = kc % 2
                    wce = p.dma("sync", wc_sl[r][:, 0:512],
                                self.wc_sp[kc * 128:(kc + 1) * 128,
                                           half * 512:half * 512 + 512],
                                waits=[sl_ev["wc"][r]], key=f"wc{r}")
                    for m4 in range(4):
                        ev = p.do("tensor", lambda kc=kc, m4=m4, r=r:
                                       nc.tensor.matmul(
                            out=self.pb[m4][:, 0:T3],
                            lhsT=wc_sl[r][:, m4 * 128:(m4 + 1) * 128],
                            rhs=hb[:, kc * T3:(kc + 1) * T3],
                            start=(kc == 0), stop=(kc == XCH - 1)),
                            waits=[wce, hbe] +
                                  (self.pwaits(self.pb[m4]) if kc == 0 else []),
                            inc=(m4 == 3 or kc == XCH - 1))
                        if kc == XCH - 1:
                            fev[m4] = ev
                    sl_ev["wc"][r] = ev
                hb_free = fev[3]
                for m4 in range(4):
                    m = half * 4 + m4
                    fte = p.do("scalar", lambda m=m, m4=m4: nc.scalar.activation(
                        out=ftmp, in_=self.pb[m4][:, 0:T3], func=AF.Tanh,
                        bias=self.fbias_f[:, m:m + 1]), waits=[fev[m4], oute],
                        inc=True)
                    self.pset(self.pb[m4], fte)
                    oute = p.do("vector", lambda m=m: nc.vector.scalar_tensor_tensor(
                        out=outf[:, m * T3:(m + 1) * T3], in0=ftmp,
                        scalar=self.mod_f[:, 5 * XCH + m:5 * XCH + m + 1],
                        in1=x2[:, m * T3:(m + 1) * T3], op0=ALU.mult, op1=ALU.add),
                        waits=[fte, outf_free], inc=True)
            if P3S <= 7:
                continue
            for s in range(NSUB):
                eev = None
                for fc4 in range(XCH // 4):
                    pbk = self.pb[fc4 % 2]
                    tev = None
                    for q in range(4):
                        fc = fc4 * 4 + q
                        tev = p.do("tensor", lambda pbk=pbk, s=s, fc=fc, q=q:
                                   nc.tensor.transpose(
                            out=pbk[:, q * 128:(q + 1) * 128],
                            in_=outf[:, fc * T3 + s * 128:fc * T3 + (s + 1) * 128],
                            identity=self.eye_f),
                            waits=[oute] + (self.pwaits(pbk) if q == 0 else []),
                            inc=(q == 3))
                    eev = p.do("vector", lambda pbk=pbk, s=s, fc4=fc4:
                               nc.vector.tensor_copy(
                        out=out_tm[:, s * D + fc4 * 512:s * D + (fc4 + 1) * 512],
                        in_=pbk[:, 0:512]), waits=[tev, st_ev[s]], inc=True)
                    self.pset(pbk, eev)
                    outf_free = tev
                out_done = p.dma("sync",
                                 self.out_h[c * T3 + s * 128:c * T3 + (s + 1) * 128, :],
                                 out_tm[:, s * D:(s + 1) * D], waits=[eev, out_done],
                                 key="outs")
                st_ev[s] = out_done
        p.wait("sync", out_done)


def build(cfg):
    mk = MK(cfg)
    return mk.build()


from concourse.bass_utils import run_bass_kernel_spmd

_CFG = dict(L=2048, T1=256, TS=512, T3=256, upto=3)
_EYE = np.eye(128, dtype=np.float32)


def kernel(**inputs):
    m = MK(dict(_CFG))
    nc = m.build()
    x = np.ascontiguousarray(inputs["L0_attn_x"], dtype=np.float32)
    c = np.ascontiguousarray(inputs["L0_c"], dtype=np.float32)
    shared = dict(
        ln1_g=inputs["L0_attn_ln1_g"], ln1_b=inputs["L0_attn_ln1_b"],
        w_in=inputs["L0_attn_w_in"], conv_w=inputs["L0_attn_conv_w"],
        conv_b=inputs["L0_attn_conv_b"], w_x=inputs["L0_attn_w_x"],
        w_dt=inputs["L0_attn_w_dt"], b_dt=inputs["L0_attn_b_dt"],
        a_log=inputs["L0_attn_a_log"], d_skip=inputs["L0_attn_d_skip"],
        w_out=inputs["L0_attn_w_out"], ln2_g=inputs["L0_attn_ln2_g"],
        ln2_b=inputs["L0_attn_ln2_b"], w_ff1=inputs["L0_attn_w_ff1"],
        b_ff1=inputs["L0_attn_b_ff1"], w_ff2=inputs["L0_attn_w_ff2"],
        b_ff2=inputs["L0_attn_b_ff2"], fw=inputs["L0_ffn_weight"],
        fa1=inputs["L0_ffn_aux_weight1"], fa2=inputs["L0_ffn_aux_weight2"],
        fbias=inputs["L0_ffn_bias"], wmod=inputs["L0_wmod"],
        eye_h=_EYE,
    )
    shared = {k: np.ascontiguousarray(v, dtype=np.float32) for k, v in shared.items()}
    in_maps = [dict(shared, x=x[i], c_vec=c[i]) for i in range(NCORES)]
    res = run_bass_kernel_spmd(nc, in_maps, core_ids=list(range(NCORES)))
    return np.stack([np.asarray(res.results[i]["out"], dtype=np.float32)
                     for i in range(NCORES)], axis=0)
